# version 15
# baseline (speedup 1.0000x reference)
import numpy as np
import concourse.bass as bass
import concourse.mybir as mybir
from concourse.bass_utils import run_bass_kernel_spmd

F32 = mybir.dt.float32
BF16 = mybir.dt.bfloat16
AF = mybir.ActivationFunctionType
ALU = mybir.AluOpType
AX = mybir.AxisListType

B, L, E, H, D = 4, 4096, 1024, 16, 64
NQ = 2048
NSLOT = 16
CTX = 4096
NEXP, FF = 8, 4096
CNT = [(2 * s) // 4 + 1 for s in range(NSLOT)]
_WIN = [int(np.ceil(128.0 / float(s))) for s in
        (2.0 ** (-np.linspace(1.0, 8.0, H, dtype=np.float32).astype(np.float64))).astype(np.float32)]
CLO = [[max(0, (256 * s - _WIN[h]) // 512) for s in range(NSLOT)]
       for h in range(H)]
MASKVAL = -1.0e36
CAP = 512
TRASH = 8 * CAP
EPS = 1e-5

_SLOPES = (2.0 ** (-np.linspace(1.0, 8.0, H, dtype=np.float32).astype(np.float64))).astype(np.float32)

ENGS = ("T", "V", "S", "Q", "G")


class Prog:
    def __init__(self):
        self.ops = {e: [] for e in ENGS}
        self.cnt = {}

    def inc(self, sem, n=1):
        self.cnt[sem] = self.cnt.get(sem, 0) + n
        return self.cnt[sem]

    def val(self, sem):
        return self.cnt.get(sem, 0)

    def emit(self, eng, fn, waits=(), inc=None):
        waits = tuple(waits)
        v = None
        if inc is not None:
            if inc[0].startswith("d"):
                waits = waits + ((inc[0], self.val(inc[0])),)
            v = self.inc(inc[0], inc[1])
        self.ops[eng].append((waits, fn, inc))
        return v


def build_program():
    nc = bass.Bass("TRN2", target_bir_lowering=False)
    P = Prog()

    d_xq = nc.dram_tensor("xq", [NQ, E], F32, kind="ExternalInput").ap()
    d_xctx = nc.dram_tensor("xctx", [CTX, E], F32, kind="ExternalInput").ap()
    d_wq = nc.dram_tensor("wq", [E, E], F32, kind="ExternalInput").ap()
    d_wk = nc.dram_tensor("wk", [E, E], F32, kind="ExternalInput").ap()
    d_wv = nc.dram_tensor("wv", [E, E], F32, kind="ExternalInput").ap()
    d_wout = nc.dram_tensor("wout", [E, E], F32, kind="ExternalInput").ap()
    d_bqr = nc.dram_tensor("bqr", [128, 8], F32, kind="ExternalInput").ap()
    d_bkr = nc.dram_tensor("bkr", [128, 8], F32, kind="ExternalInput").ap()
    d_bvb = nc.dram_tensor("bvb", [128, E], F32, kind="ExternalInput").ap()
    d_boutb = nc.dram_tensor("boutb", [128, E], F32, kind="ExternalInput").ap()
    d_w1 = nc.dram_tensor("w1", [NEXP, E, FF], F32, kind="ExternalInput").ap()
    d_w2 = nc.dram_tensor("w2", [NEXP, FF, E], F32, kind="ExternalInput").ap()
    d_b1r = nc.dram_tensor("b1r", [128, NEXP * 32], F32, kind="ExternalInput").ap()
    d_b2b = nc.dram_tensor("b2b", [NEXP, 128, E], F32, kind="ExternalInput").ap()
    d_wr = nc.dram_tensor("wr", [E, NEXP], F32, kind="ExternalInput").ap()
    d_brb = nc.dram_tensor("brb", [128, NEXP], F32, kind="ExternalInput").ap()
    d_ident = nc.dram_tensor("ident", [128, 128], F32, kind="ExternalInput").ap()
    d_bfull = nc.dram_tensor("bfull", [128, 512], F32, kind="ExternalInput").ap()
    d_bdiag = nc.dram_tensor("bdiag", [2, 128, 512], F32, kind="ExternalInput").ap()
    d_crep8 = nc.dram_tensor("crep8", [128, H * NSLOT * 8], F32, kind="ExternalInput").ap()

    d_utri = nc.dram_tensor("utri", [128, 128], F32, kind="ExternalInput").ap()
    d_onescol = nc.dram_tensor("onescol", [1, 128], F32, kind="ExternalInput").ap()
    d_onesrow = nc.dram_tensor("onesrow", [128, 1], F32, kind="ExternalInput").ap()
    d_ecvals = nc.dram_tensor("ecvals", [128, NEXP], F32, kind="ExternalInput").ap()
    d_tokids = nc.dram_tensor("tokids", [128, NSLOT], mybir.dt.int32, kind="ExternalInput").ap()
    d_xg = nc.dram_tensor("xg", [8 * CAP + 128, E], BF16)
    d_gate = nc.dram_tensor("gatehbm", [33, 128], F32)
    d_tidx = nc.dram_tensor("tidxhbm", [33, 128], mybir.dt.int32)
    d_ksc = nc.dram_tensor("ksc", [H * D, CTX], BF16)
    d_vsc = nc.dram_tensor("vsc", [H, 32, 128, D], BF16)
    d_y = nc.dram_tensor("y", [NQ, E], F32, kind="ExternalOutput").ap()

    SEMS = ["sT", "sV", "sS", "sG", "dX", "dW", "dKV", "dKVL", "dO", "dA", "dI", "bar"]

    cms = []

    def sb(name, shape, dtype):
        cm = nc.sbuf_tensor(name, shape, dtype)
        h = cm.__enter__()
        cms.append(cm)
        return h

    def psum(name, shape):
        cm = nc.psum_tensor(name, shape, F32)
        h = cm.__enter__()
        cms.append(cm)
        return h

    identb = sb("identb", [128, 128], BF16)
    smb = sb("smb", [128, 16], F32)
    dump = sb("dump", [128, 1024], F32)
    xn_sb = sb("xn_sb", [128, 1024], BF16)
    xbuf = [sb(f"xbuf{i}", [128, E], F32) for i in range(3)]
    ybuf = sb("ybuf", [128, E], F32)
    bqr_sb = sb("bqr_sb", [128, 8], F32)
    bkr_sb = sb("bkr_sb", [128, 8], F32)
    wr_sb = sb("wr_sb", [128, 8 * NEXP], BF16)
    brb_sb = sb("brb_sb", [128, NEXP], F32)
    b1_sb = sb("b1_sb", [128, NEXP * 32], F32)
    s_all = sb("s_all", [128, NSLOT * NEXP], F32)
    logit_sb = sb("logit_sb", [128, NEXP], F32)
    e8_sb = sb("e8_sb", [128, NEXP], F32)
    g8_sb = sb("g8_sb", [128, 1], F32)
    m8_sb = sb("m8_sb", [128, 1], F32)
    ge_sb = sb("ge_sb", [128, 1], F32)
    sel_sb = sb("sel_sb", [128, 1], F32)
    ntk_sb = sb("ntk_sb", [128, 1], F32)
    mc_sb = [sb(f"mc{i}", [128, 8], F32) for i in range(2)]
    negm_sb = [sb(f"negm{i}", [128, 1], F32) for i in range(2)]
    biasb = [sb(f"biasb{i}", [128, 8], F32) for i in range(2)]
    lacc = [sb(f"lacc{i}", [128, 8], F32) for i in range(2)]
    lsum = [sb(f"lsum{i}", [128, 1], F32) for i in range(2)]
    rl_sb = [sb(f"rl{i}", [128, 1], F32) for i in range(2)]
    dump8 = sb("dump8", [128, 8], F32)
    utri_sb = sb("utri_sb", [128, 128], F32)
    onescol_sb = sb("onescol_sb", [1, 128], F32)
    onesrow_sb = sb("onesrow_sb", [128, 1], F32)
    ecvals_sb = sb("ecvals_sb", [128, NEXP], F32)
    tokids_sb = sb("tokids_sb", [128, NSLOT], mybir.dt.int32)
    oh_sb = sb("oh_sb", [128, NEXP], F32)
    totals_sb = sb("totals_sb", [1, NEXP], F32)
    tmp8_sb = sb("tmp8_sb", [128, NEXP], F32)
    posown_sb = sb("posown_sb", [128, 1], F32)
    ec_sb = sb("ec_sb", [128, 1], F32)
    val_sb = sb("val_sb", [128, 1], F32)
    valt_sb = sb("valt_sb", [128, 1], F32)
    slotf_sb = sb("slotf_sb", [128, 1], F32)
    sloti_sb = sb("sloti_sb", [128, 1], mybir.dt.int32)
    gate_sb = sb("gate_sb", [128, 32], F32)
    tidx_sb = sb("tidx_sb", [128, 32], mybir.dt.int32)
    ji_sb = sb("ji_sb", [128, 1], mybir.dt.int32)
    pi_sb = sb("pi_sb", [128, 1], mybir.dt.int32)
    slotti_sb = sb("slotti_sb", [128, 1], mybir.dt.int32)
    vali_sb = sb("vali_sb", [128, 1], mybir.dt.int32)

    SBUF_TOP = 16512 + 212863
    rem = nc.sbuf_bytes_remaining
    mb = SBUF_TOP - rem
    mb = (mb + 511) // 512 * 512
    need = 184320
    assert mb + need <= SBUF_TOP - 256, (mb, rem)

    def at(name, off, shape, dtype):
        return nc.alloc_sbuf_tensor_at(name, shape, dtype, offset=mb + off)

    wk_sb = at("wk_sb", 0, [128, 8 * E], BF16)
    wv_sb = at("wv_sb", 16384, [128, 8 * E], BF16)
    xnt_g = [at("xntg0", 32768, [128, 8 * 512], BF16),
             at("xntg1", 40960, [128, 8 * 512], BF16)]
    kstage = [at("kstage0", 49152, [128, 512], BF16),
              at("kstage1", 50176, [128, 512], BF16)]
    vstage = [at("vstage0", 51200, [128, 512], BF16),
              at("vstage1", 52224, [128, 512], BF16)]
    bvb_sb = at("bvb_sb", 53248, [128, E], F32)
    qT = at("qT", 0, [128, 8 * NQ], BF16)
    attnT = at("attnT", 32768, [128, 8 * NQ], BF16)
    wq_sb = at("wq_sb", 65536, [128, 8 * E], BF16)
    xn1qT = at("xn1qT", 81920, [128, 8 * NQ], BF16)
    kh_sb = at("kh_sb", 65536, [128, CTX], BF16)
    vh_sb = [at("vh0", 73728, [128, 32 * D], BF16),
             at("vh1", 77824, [128, 32 * D], BF16)]
    S_sb = [at("Ssb0", 81920, [128, CTX], F32),
            at("Ssb1", 98304, [128, CTX], F32)]
    Pt = [at("Pt0", 114688, [128, CTX], BF16),
          at("Pt1", 122880, [128, CTX], BF16),
          at("Pt2", 131072, [128, CTX], BF16)]
    PT = at("PT", 139264, [128, 32 * 128], BF16)
    crep_sb = at("crep_sb", 147456, [128, H * NSLOT * 8], F32)
    bfull_sb = at("bfull_sb", 155648, [128, 512], F32)
    bdiag_sb = at("bdiag_sb", 157696, [128, 2 * 512], F32)
    wout_sb = at("wout_sb", 65536, [128, 8 * E], BF16)
    boutb_sb = at("boutb_sb", 81920, [128, E], F32)
    resout2 = at("resout2", 98304, [128, E], F32)
    ytmp2 = at("ytmp2", 12288, [128, E], F32)
    xn2T = at("xn2T", 0, [128, 8 * NQ], BF16)
    xg_sb = [at("xg0", 0, [128, E], BF16), at("xg1", 2048, [128, E], BF16),
             at("xg2", 8192, [128, E], BF16), at("xg3", 10240, [128, E], BF16)]
    xgT = at("xgT", 4096, [128, 8 * 256], BF16)
    w1_sb = at("w1_sb", 32768, [128, 8 * FF], BF16)
    w2_sb = at("w2_sb", 98304, [128, 32 * E], BF16)
    hT_sb = at("hT_sb", 163840, [128, 32 * 256], BF16)
    b2b_sb = at("b2b_sb", 180224, [128, E], F32)

    psA = psum("psA", [128, 512])
    psB = psum("psB", [128, 512])
    psC = psum("psC", [128, 512])
    psD = psum("psD", [128, 512])
    psE = psum("psE", [128, 512])
    psF = psum("psF", [128, 512])
    psO = [psum(f"psO{i}", [128, 128]) for i in range(2)]


    def barrier():
        tgt = P.val("bar") + 5
        for e in ENGS:
            if e == "Q":
                P.emit(e, lambda eng: eng.nop(), inc=("bar", 1))
            else:
                P.emit(e, lambda eng: eng.drain(), inc=("bar", 1))
        for e in ENGS:
            P.emit(e, None, waits=(("bar", tgt),))

    def dr(eng):
        P.emit(eng, lambda x: x.drain())

    def ln_stats(src_ap, par):
        c = 8 * par
        s_sum = smb[:, c + 0:c + 1]
        s_sq = smb[:, c + 1:c + 2]
        s_mean = smb[:, c + 2:c + 3]
        s_v = smb[:, c + 3:c + 4]
        s_nmr = smb[:, c + 5:c + 6]
        vq = P.emit("S", lambda s: s.activation(dump[:, :], src_ap, AF.Square,
                                                accum_out=s_sq), inc=("sS", 1))
        P.emit("V", lambda v: v.reduce_sum(s_sum, src_ap, axis=AX.X))
        dr("V")
        P.emit("V", lambda v: v.tensor_scalar(s_mean, s_sum, 1.0 / E, None,
                                              op0=ALU.mult))
        P.emit("V", lambda v: v.tensor_scalar(s_v, s_sq, 1.0 / E, EPS,
                                              op0=ALU.mult, op1=ALU.add),
               waits=(("sS", vq),))
        dr("V")
        P.emit("V", lambda v: v.tensor_tensor(s_nmr, s_mean, s_mean, op=ALU.mult))
        dr("V")
        P.emit("V", lambda v: v.tensor_tensor(s_v, s_v, s_nmr, op=ALU.subtract))
        dr("V")
        P.emit("V", lambda v: v.reciprocal(s_v, s_v))
        return P.emit("V", lambda v: v.engine_nop(), inc=("sV", 1))

    def ln_finish(src_ap, dst_ap, par, vst):
        c = 8 * par
        s_mean = smb[:, c + 2:c + 3]
        s_v = smb[:, c + 3:c + 4]
        s_rstd = smb[:, c + 4:c + 5]
        s_nmr = smb[:, c + 5:c + 6]
        sq = P.emit("S", lambda s: s.activation(s_rstd, s_v, AF.Sqrt),
                    waits=(("sV", vst),), inc=("sS", 1))
        P.emit("V", lambda v: v.tensor_scalar(s_nmr, s_mean, s_rstd, -1.0,
                                              op0=ALU.mult, op1=ALU.mult),
               waits=(("sS", sq),))
        vfin = P.emit("V", lambda v: v.engine_nop(), inc=("sV", 1))
        dr("S")
        return P.emit("S", lambda s: s.activation(dst_ap, src_ap, AF.Identity,
                                                  bias=s_nmr, scale=s_rstd),
                      waits=(("sV", vfin),), inc=("sS", 1))

    def ln_tile(src_ap, dst_ap):
        return ln_finish(src_ap, dst_ap, 0, ln_stats(src_ap, 0))

    def transpose_1024(src_bf16, dst_tensor, dst_off, dst_row_stride):
        sev = None
        for half in range(2):
            ps = psC if half == 0 else psD
            tev = None
            for j in range(4):
                fc = half * 4 + j
                tev = P.emit("T", lambda tt, ps=ps, j=j, fc=fc, src=src_bf16: tt.matmul(
                    ps[:, j * 128:(j + 1) * 128],
                    src[:, fc * 128:(fc + 1) * 128], identb[:, :],
                    start=True, stop=True),
                    waits=(("sS", P.val("sS")),) if j == 0 else (),
                    inc=("sT", 1) if j == 3 else None)
            dst = bass.AP(dst_tensor, half * 4 * dst_row_stride + dst_off,
                          [[8 * dst_row_stride, 128], [dst_row_stride, 4], [1, 128]])
            sev = P.emit("S", lambda s, ps=ps, dst=dst: s.activation(
                dst, ps[:, :], AF.Identity), waits=(("sT", tev),), inc=("sS", 1))
        return sev

    def wload(dst, src):
        P.emit("G", lambda g, a=dst, b_=src: g.dma_start(
            out=a[:, :].rearrange("p (c f) -> p c f", c=8),
            in_=b_.rearrange("(c p) f -> p c f", p=128)), inc=("dW", 16))

    P.emit("G", lambda g: g.dma_start(out=identb[:, :], in_=d_ident), inc=("dW", 16))
    wload(wk_sb, d_wk)
    wload(wv_sb, d_wv)
    wload(wq_sb, d_wq)
    P.emit("G", lambda g: g.dma_start(
        out=wr_sb[:, :].rearrange("p (c f) -> p c f", c=8),
        in_=d_wr.rearrange("(c p) f -> p c f", p=128)), inc=("dW", 16))
    for ap_sb, ap_d in ((bqr_sb, d_bqr), (bkr_sb, d_bkr), (bvb_sb, d_bvb),
                        (brb_sb, d_brb), (bfull_sb, d_bfull),
                        (b1_sb, d_b1r), (crep_sb, d_crep8)):
        P.emit("Q", lambda q, a=ap_sb, b_=ap_d: q.dma_start(out=a[:, :], in_=b_),
               inc=("dX", 16))
    P.emit("Q", lambda q: q.dma_start(
        out=bdiag_sb[:, :].rearrange("p (s c) -> p s c", s=2),
        in_=d_bdiag.transpose([1, 0, 2])), inc=("dX", 16))
    for ap_sb, ap_d in ((utri_sb, d_utri), (onescol_sb, d_onescol),
                        (onesrow_sb, d_onesrow), (ecvals_sb, d_ecvals),
                        (tokids_sb, d_tokids)):
        P.emit("Q", lambda q, a=ap_sb, b_=ap_d: q.dma_start(out=a[:, :], in_=b_),
               inc=("dX", 16))
    w_const = P.val("dW")
    x_const = P.val("dX")
    P.emit("T", None, waits=(("dW", w_const), ("dX", x_const)))
    P.emit("V", None, waits=(("dX", x_const),))
    P.emit("S", None, waits=(("dX", x_const),))

    ln_done = {}
    ld = {}
    vst1 = {}
    xng_free = {}
    for t in range(3):
        ld[t] = P.emit("Q", lambda q, t=t, bs=t % 3: q.dma_start(
            out=xbuf[bs][:, :], in_=d_xctx[t * 128:(t + 1) * 128, :]),
            inc=("dX", 16))

    def s1_kv(g, xn):
        for hp in range(8):
            ps = psA if hp % 2 == 0 else psB
            mm = None
            for ech in range(8):
                mm = P.emit("T", lambda tt, ps=ps, ech=ech, hp=hp, xn=xn: tt.matmul(
                    ps[:, :], wk_sb[:, ech * E + hp * 128: ech * E + (hp + 1) * 128],
                    xn[:, ech * 512:(ech + 1) * 512],
                    start=(ech == 0), stop=(ech == 7)),
                    waits=(("sS", P.val("sS")),) if ech == 0 else (),
                    inc=("sT", 1) if ech == 7 else None)
            st = kstage[hp % 2]
            sev = P.emit("S", lambda s, ps=ps, st=st, hp=hp: s.activation(
                st[:, :], ps[:, :], AF.Identity, bias=bkr_sb[:, hp:hp + 1]),
                waits=(("sT", mm), ("dKV", max(0, P.val("dKV") - 16))),
                inc=("sS", 1))
            P.emit("Q", lambda q, st=st, hp=hp, g=g: q.dma_start(
                out=bass.AP(d_ksc, hp * 128 * CTX + g * 512,
                            [[CTX, 128], [1, 512]]),
                in_=st[:, :]), waits=(("sS", sev),), inc=("dKV", 16))
        for ti in range(4):
            for hh in range(2):
                ps = psE if hh == 0 else psF
                mm = None
                for ech in range(8):
                    mm = P.emit("T", lambda tt, ps=ps, ech=ech, hh=hh, ti=ti, xn=xn: tt.matmul(
                        ps[:, :], xn[:, ech * 512 + ti * 128: ech * 512 + (ti + 1) * 128],
                        wv_sb[:, ech * E + hh * 512: ech * E + (hh + 1) * 512],
                        start=(ech == 0), stop=(ech == 7)),
                        waits=(("sV", P.val("sV")),) if ech == 0 else (),
                        inc=("sT", 1) if ech == 7 else None)
                st = vstage[hh]
                vev = P.emit("V", lambda v, ps=ps, st=st, hh=hh: v.tensor_tensor(
                    st[:, :], ps[:, :], bvb_sb[:, hh * 512:(hh + 1) * 512],
                    op=ALU.add),
                    waits=(("sT", mm), ("dKV", max(0, P.val("dKV") - 16))),
                    inc=("sV", 1))
                P.emit("Q", lambda q, st=st, hh=hh, g=g, ti=ti: q.dma_start(
                    out=bass.AP(d_vsc, (hh * 8 * 32 + g * 4 + ti) * 128 * D,
                                [[D, 128], [32 * 128 * D, 8], [1, D]]),
                    in_=st[:, :].rearrange("p (h d) -> p h d", h=8)),
                    waits=(("sV", vev),), inc=("dKV", 16))

    def s1_finish(u):
        ev = ln_finish(xbuf[u % 3][:, :], xn_sb[:, :], u % 2, vst1[u])
        ln_done[u] = ev
        if u + 3 < 32:
            ld[u + 3] = P.emit("Q", lambda q, t2=u + 3, bs=(u + 3) % 3: q.dma_start(
                out=xbuf[bs][:, :], in_=d_xctx[t2 * 128:(t2 + 1) * 128, :]),
                waits=(("sS", ev),), inc=("dX", 16))
        g, ti = u // 4, u % 4
        P.emit("T", None, waits=(("sS", ev), xng_free[g]))
        transpose_1024(xn_sb, xnt_g[g % 2], ti * 128, 512)
        if ti == 3:
            s1_kv(g, xnt_g[g % 2])

    for t in range(32):
        if t % 4 == 0:
            xng_free[t // 4] = ("sT", P.val("sT"))
        P.emit("S", None, waits=(("dX", ld[t]),))
        P.emit("V", None, waits=(("dX", ld[t]),))
        if t == 1:
            dr("S")
        vst1[t] = ln_stats(xbuf[t % 3][:, :], t % 2)
        if t >= 1:
            s1_finish(t - 1)
    s1_finish(31)
    kv_done = ("dKV", P.val("dKV"))

    lnq = {}
    vstq = {}
    ldq = {}

    def s2a_finish(u):
        ev = ln_finish(xbuf[u % 3][:, :], xn_sb[:, :], u % 2, vstq[u])
        lnq[u] = ev
        if u + 3 < NSLOT:
            ldq[u + 3] = P.emit("Q", lambda q, t=u + 3, bs=(u + 3) % 3: q.dma_start(
                out=xbuf[bs][:, :], in_=d_xq[t * 128:(t + 1) * 128, :]),
                waits=(("sS", ev),), inc=("dX", 16))
        P.emit("T", None, waits=(("sS", ev),))
        transpose_1024(xn_sb, xn1qT, u * 128, NQ)

    for t in range(3):
        ldq[t] = P.emit("Q", lambda q, t=t, bs=t % 3: q.dma_start(
            out=xbuf[bs][:, :], in_=d_xq[t * 128:(t + 1) * 128, :]),
            waits=(("sS", ln_done[[30, 31, 29][t]]),), inc=("dX", 16))
    for t in range(NSLOT):
        P.emit("S", None, waits=(("dX", ldq[t]),))
        P.emit("V", None, waits=(("dX", ldq[t]),))
        if t == 1:
            dr("S")
        vstq[t] = ln_stats(xbuf[t % 3][:, :], t % 2)
        if t >= 1:
            s2a_finish(t - 1)
    s2a_finish(NSLOT - 1)
    for g in range(4):
        for hp in range(8):
            ps = psA if hp % 2 == 0 else psB
            mm = None
            for ech in range(8):
                mm = P.emit("T", lambda tt, ps=ps, ech=ech, hp=hp, g=g: tt.matmul(
                    ps[:, :], wq_sb[:, ech * E + hp * 128: ech * E + (hp + 1) * 128],
                    xn1qT[:, ech * NQ + g * 512: ech * NQ + (g + 1) * 512],
                    start=(ech == 0), stop=(ech == 7)),
                    waits=(("sS", P.val("sS")),) if ech == 0 else (),
                    inc=("sT", 1) if ech == 7 else None)
            P.emit("S", lambda s, ps=ps, hp=hp, g=g: s.activation(
                qT[:, hp * NQ + g * 512: hp * NQ + (g + 1) * 512],
                ps[:, :], AF.Identity, bias=bqr_sb[:, hp:hp + 1]),
                waits=(("sT", mm),), inc=("sS", 1))

    qproj_T = P.val("sT")
    qproj_S = P.val("sS")
    P.emit("T", None, waits=(("sS", qproj_S),))

    NIT = H * NSLOT

    evA_bias = [0] * NIT
    evB_exp = [0] * NIT
    evB_rl = [0] * NIT
    evB_norm = [0] * NIT
    evC_T = [0] * NIT
    evC_O = [0] * NIT
    ev_kvload = [0] * (H + 1)
    ev_headlastT = [0] * H

    def load_head(h):
        wt = []
        if h >= 2:
            wt.append(("sT", ev_headlastT[h - 2]))
        if h == 0:
            wt.append(kv_done)
            wt.append(("sT", qproj_T))
        P.emit("Q", lambda q, h=h: q.dma_start(
            out=kh_sb[(h % 2) * 64:(h % 2) * 64 + 64, :],
            in_=bass.AP(d_ksc, h * 64 * CTX, [[CTX, 64], [1, CTX]])),
            waits=tuple(wt), inc=("dKVL", 16))
        P.emit("Q", lambda q, h=h: q.dma_start(
            out=vh_sb[h % 2][:, :].rearrange("p (t d) -> p t d", t=32),
            in_=bass.AP(d_vsc, h * 32 * 128 * D,
                        [[D, 128], [128 * D, 32], [1, D]])),
            inc=("dKVL", 16))
        ev_kvload[h] = P.val("dKVL")

    load_head(0)
    load_head(1)

    gchunk = [0]
    chunk_stt_ev = {}
    gtg = [0]
    tg_evac_ev = {}

    def stageA(i):
        h, s = divmod(i, NSLOT)
        cnt = CNT[s]
        hp, hb = h // 2, (h % 2) * 64
        par = i % 2
        if s == 3 and h + 1 < H:
            load_head(h + 1)
        cbase = (h * NSLOT + s) * 8
        c_lo = CLO[h][s]
        for c in range(c_lo, cnt):
            gc = gchunk[0]
            ps = (psA, psB, psE, psF)[gc % 4]
            wt = [("sV", chunk_stt_ev.get(gc - 4, 0))]
            if c == c_lo:
                wt.append(("dKVL", ev_kvload[h]))
                if i >= 2:
                    wt.append(("sS", evB_exp[i - 2]))
            P.emit("T", lambda tt, ps=ps, c=c, hp=hp, hb=hb, s=s: tt.matmul(
                ps[:, :],
                qT[hb:hb + 64, hp * NQ + s * 128: hp * NQ + (s + 1) * 128],
                kh_sb[hb:hb + 64, c * 512:(c + 1) * 512],
                start=True, stop=True),
                waits=tuple(wt), inc=("sT", 1))
            tval = P.val("sT")
            btile = (bfull_sb[:, :] if c < cnt - 1
                     else bdiag_sb[:, (s % 2) * 512:(s % 2 + 1) * 512])
            sc = float(8.0 * _SLOPES[h])
            P.emit("V", lambda v, b=btile, sc=sc, par=par, c=c, ps=ps: v.scalar_tensor_tensor(
                S_sb[par][:, c * 512:(c + 1) * 512], b, sc, ps[:, :],
                op0=ALU.mult, op1=ALU.add),
                waits=(("sT", tval),))
            chunk_stt_ev[gc] = P.emit("V", lambda v: v.engine_nop(), inc=("sV", 1))
            dr("V")
            P.emit("V", lambda v, par=par, c=c: v.reduce_max(
                mc_sb[par][:, c:c + 1], S_sb[par][:, c * 512:(c + 1) * 512],
                axis=AX.X))
            gchunk[0] += 1
        dr("V")
        P.emit("V", lambda v, par=par, cnt=cnt, cbase=cbase, c_lo=c_lo: v.tensor_tensor(
            mc_sb[par][:, c_lo:cnt], mc_sb[par][:, c_lo:cnt],
            crep_sb[:, cbase + c_lo:cbase + cnt], op=ALU.add))
        dr("V")
        P.emit("V", lambda v, par=par, cnt=cnt, c_lo=c_lo: v.reduce_max(
            negm_sb[par][:, :], mc_sb[par][:, c_lo:cnt], axis=AX.X, negate=True))
        dr("V")
        wtb = (("sS", evB_exp[i - 2]),) if i >= 2 else ()
        for c in range(c_lo, cnt):
            P.emit("V", lambda v, par=par, c=c, cbase=cbase: v.tensor_scalar(
                biasb[par][:, c:c + 1], negm_sb[par][:, :],
                crep_sb[:, cbase + c:cbase + c + 1], 0.125,
                op0=ALU.add, op1=ALU.mult),
                waits=wtb if c == c_lo else ())
        evA_bias[i] = P.emit("V", lambda v: v.engine_nop(), inc=("sV", 1))

    def stageB1(i):
        h, s = divmod(i, NSLOT)
        cnt = CNT[s]
        c_lo = CLO[h][s]
        par = i % 2
        p3 = i % 3
        wt = [("sV", evA_bias[i])]
        if i >= 3:
            wt.append(("sT", evC_T[i - 3]))
        for c in range(c_lo, cnt):
            P.emit("S", lambda s_, par=par, p3=p3, c=c: s_.activation(
                Pt[p3][:, c * 512:(c + 1) * 512],
                S_sb[par][:, c * 512:(c + 1) * 512],
                AF.Exp, bias=biasb[par][:, c:c + 1], scale=0.125,
                accum_out=lacc[par][:, c:c + 1]),
                waits=tuple(wt) if c == c_lo else ())
        wl = (("sV", evB_rl[i - 2]),) if i >= 2 else ()
        dr("S")
        evB_exp[i] = P.emit("S", lambda s_, par=par, cnt=cnt, c_lo=c_lo: s_.activation(
            dump8[:, c_lo:cnt], lacc[par][:, c_lo:cnt], AF.Identity,
            accum_out=lsum[par][:, :]), waits=wl, inc=("sS", 1))

    def stageB2(i):
        par = i % 2
        p3 = i % 3
        wt = [("sS", evB_exp[i])]
        evB_rl[i] = P.emit("V", lambda v, par=par: v.reciprocal(
            rl_sb[par][:, :], lsum[par][:, :]), waits=tuple(wt), inc=("sV", 1))
        h, s = divmod(i, NSLOT)
        lo = CLO[h][s] * 512
        ctx = CNT[s] * 512
        dr("V")
        evB_norm[i] = P.emit("V", lambda v, p3=p3, par=par, lo=lo, ctx=ctx: v.tensor_scalar(
            Pt[p3][:, lo:ctx], Pt[p3][:, lo:ctx], rl_sb[par][:, :], None,
            op0=ALU.mult), inc=("sV", 1))

    def stageC(i):
        h, s = divmod(i, NSLOT)
        cnt = CNT[s]
        c_lo = CLO[h][s]
        nks = cnt * 4
        p3 = i % 3
        opar = i % 2
        hb = (h % 2) * 64
        hp = h // 2
        evac_of_tg = {}

        av_last = [0]

        def emit_av(tg):
            wtav = [("sS", evac_of_tg[tg])]
            if tg == c_lo and i >= 2:
                wtav.append(("sS", evC_O[i - 2]))
            for j in range(4):
                ks = tg * 4 + j
                last = (ks == nks - 1)
                ev = P.emit("T", lambda tt, ks=ks, hb=hb, opar=opar, h=h,
                       first=(tg == c_lo and j == 0),
                       last=last: tt.matmul(
                    psO[opar][hb:hb + 64, :],
                    vh_sb[h % 2][:, ks * 64:(ks + 1) * 64],
                    PT[:, ks * 128:(ks + 1) * 128],
                    start=first, stop=last, tile_position=(0, hb)),
                    waits=tuple(wtav) if j == 0 else (),
                    inc=("sT", 1) if last else None)
                if last:
                    av_last[0] = ev

        for tg in range(c_lo, cnt):
            gt = gtg[0]
            ps = psC if gt % 2 == 0 else psD
            wt = [("sV", evB_norm[i]), ("sS", tg_evac_ev.get(gt - 2, 0))]
            trev = None
            for j in range(4):
                ks = tg * 4 + j
                trev = P.emit("T", lambda tt, ps=ps, j=j, ks=ks, p3=p3: tt.matmul(
                    ps[:, j * 128:(j + 1) * 128],
                    Pt[p3][:, ks * 128:(ks + 1) * 128], identb[:, :],
                    start=True, stop=True),
                    waits=tuple(wt) if j == 0 else (),
                    inc=("sT", 1) if j == 3 else None)
            sev = P.emit("S", lambda s_, ps=ps, tg=tg: s_.activation(
                PT[:, tg * 512:(tg + 1) * 512], ps[:, :], AF.Identity),
                waits=(("sT", trev),), inc=("sS", 1))
            tg_evac_ev[gt] = sev
            evac_of_tg[tg] = sev
            gtg[0] += 1
            if tg >= c_lo + 1:
                emit_av(tg - 1)
        emit_av(cnt - 1)
        evC_T[i] = av_last[0]
        if s == NSLOT - 1:
            ev_headlastT[h] = evC_T[i]
        evC_O[i] = P.emit("S", lambda s_, hb=hb, opar=opar, hp=hp, s=s: s_.activation(
            attnT[hb:hb + 64, hp * NQ + s * 128: hp * NQ + (s + 1) * 128],
            psO[opar][hb:hb + 64, 0:128], AF.Identity),
            waits=(("sT", evC_T[i]),), inc=("sS", 1))

    for r in range(NIT + 3):
        if r < NIT:
            stageA(r)
        if 1 <= r <= NIT:
            stageB1(r - 1)
        if 2 <= r <= NIT + 1:
            stageB2(r - 2)
        if 3 <= r <= NIT + 2:
            stageC(r - 3)

    attn_T = P.val("sT")
    attn_S = P.val("sS")
    attn_V = P.val("sV")

    out_ev = {}
    P.emit("G", lambda g: g.dma_start(
        out=wout_sb[:, :].rearrange("p (c f) -> p c f", c=8),
        in_=d_wout.rearrange("(c p) f -> p c f", p=128)),
        waits=(("sT", attn_T),), inc=("dW", 16))
    P.emit("Q", lambda q: q.dma_start(out=boutb_sb[:, :], in_=d_boutb),
           waits=(("sV", attn_V),), inc=("dX", 16))
    wout_ld = P.val("dW")
    boutb_ld = P.val("dX")
    for t in range(NSLOT):
        lev = P.emit("Q", lambda q, t=t, bs=t % 3: q.dma_start(
            out=xbuf[bs][:, :], in_=d_xq[t * 128:(t + 1) * 128, :]),
            waits=(("sV", P.val("sV")),) if t >= 3 else (), inc=("dX", 16))
        for half in range(2):
            ps = psE if half == 0 else psF
            mm = None
            for fc in range(8):
                wt = ()
                if fc == 0:
                    wt = (("sV", P.val("sV")), ("dW", wout_ld),
                          ("sS", attn_S))
                mm = P.emit("T", lambda tt, ps=ps, fc=fc, t=t, half=half: tt.matmul(
                    ps[:, :],
                    attnT[:, fc * NQ + t * 128: fc * NQ + (t + 1) * 128],
                    wout_sb[:, fc * E + half * 512: fc * E + (half + 1) * 512],
                    start=(fc == 0), stop=(fc == 7)),
                    waits=wt, inc=("sT", 1) if fc == 7 else None)
            P.emit("V", lambda v, ps=ps, half=half: v.tensor_tensor(
                dump[:, half * 512:(half + 1) * 512], ps[:, :],
                boutb_sb[:, half * 512:(half + 1) * 512], op=ALU.add),
                waits=(("sT", mm), ("dX", boutb_ld)))
        dr("V")
        ro = ybuf if t % 2 == 0 else resout2
        vev = P.emit("V", lambda v, bs=t % 3, ro=ro: v.tensor_tensor(
            ro[:, :], dump[:, :], xbuf[bs][:, :], op=ALU.add),
            waits=(("dX", lev), ("dO", max(0, P.val("dO") - 16))), inc=("sV", 1))
        out_ev[t] = P.emit("Q", lambda q, t=t, ro=ro: q.dma_start(
            out=d_y[t * 128:(t + 1) * 128, :], in_=ro[:, :]),
            waits=(("sV", vev),), inc=("dO", 16))
    out_written = ("dO", P.val("dO"))

    zm = P.emit("V", lambda v: v.memset(ybuf[:, :], 0.0),
                waits=(out_written,))
    zm = P.emit("V", lambda v: v.drain(), inc=("sV", 1))
    for j in range(33):
        P.emit("G", lambda g, j=j: g.dma_start(
            out=bass.AP(d_xg, j * 128 * E, [[E, 128], [1, E]]),
            in_=ybuf[:, :].bitcast(BF16)[:, 0:E]),
            waits=(("sV", zm),) if j == 0 else (), inc=("dW", 16))
    P.emit("G", lambda g: g.dma_start(
        out=bass.AP(d_gate, 0, [[128, 33], [1, 128]]),
        in_=ybuf[0:33, 0:128]), inc=("dW", 16))
    zf_ev = P.emit("G", lambda g: g.dma_start(
        out=bass.AP(d_tidx, 0, [[128, 33], [1, 128]]),
        in_=ybuf[:, :].bitcast(mybir.dt.int32)[0:33, 0:128]), inc=("dW", 16))
    s2c_V = P.val("sV")
    lnr = {}
    for t in range(NSLOT):
        wt = [("sS", lnr[t - 3])] if t >= 3 else [("sV", s2c_V)]
        wt.append(("dO", out_ev[t]))
        lev = P.emit("Q", lambda q, t=t, bs=t % 3: q.dma_start(
            out=xbuf[bs][:, :], in_=d_y[t * 128:(t + 1) * 128, :]),
            waits=tuple(wt), inc=("dX", 16))
        P.emit("S", None, waits=(("dX", lev),) + ((("sV", s2c_V),) if t == 0 else ()))
        P.emit("V", None, waits=(("dX", lev),))
        ev = ln_tile(xbuf[t % 3][:, :], xn_sb[:, :])
        lnr[t] = ev
        P.emit("T", None, waits=(("sS", ev),))
        transpose_1024(xn_sb, xn2T, t * 128, NQ)
        mm = None
        for ech in range(8):
            mm = P.emit("T", lambda tt, ech=ech, t=t: tt.matmul(
                psO[0][:, 0:NEXP],
                xn2T[:, ech * NQ + t * 128: ech * NQ + (t + 1) * 128],
                wr_sb[:, ech * NEXP:(ech + 1) * NEXP],
                start=(ech == 0), stop=(ech == 7)),
                waits=(("sS", P.val("sS")), ("sV", P.val("sV"))) if ech == 0 else (),
                inc=("sT", 1) if ech == 7 else None)
        P.emit("V", lambda v: v.tensor_tensor(
            logit_sb[:, :], psO[0][:, 0:NEXP], brb_sb[:, :], op=ALU.add),
            waits=(("sT", mm), ("dI", P.val("dI"))))
        dr("V")
        P.emit("V", lambda v: v.reduce_max(m8_sb[:, :], logit_sb[:, :],
                                           axis=AX.X, negate=True))
        vv = P.emit("V", lambda v: v.engine_nop(), inc=("sV", 1))
        sv = P.emit("S", lambda s: s.activation(
            e8_sb[:, :], logit_sb[:, :], AF.Exp, bias=m8_sb[:, :],
            accum_out=g8_sb[:, :]), waits=(("sV", vv),), inc=("sS", 1))
        P.emit("V", lambda v: v.reciprocal(g8_sb[:, :], g8_sb[:, :]),
               waits=(("sS", sv),))
        P.emit("V", lambda v: v.tensor_scalar(
            oh_sb[:, :], logit_sb[:, :], m8_sb[:, :], None, op0=ALU.add))
        dr("V")
        P.emit("V", lambda v: v.tensor_scalar(
            oh_sb[:, :], oh_sb[:, :], 0.0, None, op0=ALU.is_ge))
        dr("V")
        oh_done = P.emit("V", lambda v: v.engine_nop(), inc=("sV", 1))
        if t == 0:
            P.emit("V", lambda v: v.memset(totals_sb[:, :], 0.0))
            tot_ready = P.emit("V", lambda v: v.drain(), inc=("sV", 1))
        P.emit("T", lambda tt: tt.matmul(
            psO[1][:, 0:NEXP], utri_sb[:, :], oh_sb[:, :],
            start=True, stop=False),
            waits=(("sV", oh_done), ("sV", P.val("sV"))))
        posmm = P.emit("T", lambda tt: tt.matmul(
            psO[1][:, 0:NEXP], onescol_sb[:, :], totals_sb[:, :],
            start=False, stop=True),
            waits=(("sV", tot_ready),), inc=("sT", 1))
        colsum_ev = P.emit("T", lambda tt: tt.matmul(
            psE[0:1, 0:NEXP], onesrow_sb[:, :], oh_sb[:, :],
            start=True, stop=True),
            waits=(("sV", tot_ready),), inc=("sT", 1))
        tot_ready = P.emit("V", lambda v: v.tensor_tensor(
            totals_sb[:, :], totals_sb[:, :], psE[0:1, 0:NEXP], op=ALU.add),
            waits=(("sT", colsum_ev),), inc=("sV", 1))
        wsl = [("sT", posmm)]
        if t >= 1:
            wsl.append(("dI", scat_ev))
        P.emit("V", lambda v: v.tensor_tensor(
            tmp8_sb[:, :], psO[1][:, 0:NEXP], oh_sb[:, :], op=ALU.mult),
            waits=tuple(wsl))
        dr("V")
        P.emit("V", lambda v: v.reduce_sum(posown_sb[:, :], tmp8_sb[:, :],
                                           axis=AX.X))
        dr("V")
        P.emit("V", lambda v: v.tensor_tensor(
            tmp8_sb[:, :], oh_sb[:, :], ecvals_sb[:, :], op=ALU.mult))
        dr("V")
        P.emit("V", lambda v: v.reduce_sum(ec_sb[:, :], tmp8_sb[:, :],
                                           axis=AX.X))
        dr("V")
        P.emit("V", lambda v: v.tensor_scalar(
            val_sb[:, :], posown_sb[:, :], float(CAP), None, op0=ALU.is_lt))
        P.emit("V", lambda v: v.tensor_tensor(
            slotf_sb[:, :], posown_sb[:, :], ec_sb[:, :], op=ALU.add))
        dr("V")
        P.emit("V", lambda v: v.tensor_scalar(
            valt_sb[:, :], val_sb[:, :], -float(TRASH), float(TRASH),
            op0=ALU.mult, op1=ALU.add))
        P.emit("V", lambda v: v.tensor_tensor(
            slotf_sb[:, :], slotf_sb[:, :], val_sb[:, :], op=ALU.mult))
        dr("V")
        P.emit("V", lambda v: v.tensor_tensor(
            slotf_sb[:, :], slotf_sb[:, :], valt_sb[:, :], op=ALU.add))
        dr("V")
        P.emit("V", lambda v: v.tensor_copy(
            sloti_sb[:, :], slotf_sb[:, :]))
        P.emit("V", lambda v: v.tensor_copy(vali_sb[:, :], val_sb[:, :]))
        dr("V")
        P.emit("V", lambda v: v.tensor_scalar(
            ji_sb[:, :], sloti_sb[:, :], 7, None, op0=ALU.arith_shift_right))
        P.emit("V", lambda v: v.tensor_scalar(
            pi_sb[:, :], sloti_sb[:, :], 127, None, op0=ALU.bitwise_and))
        dr("V")
        P.emit("V", lambda v: v.tensor_scalar(
            pi_sb[:, :], pi_sb[:, :], 32, None, op0=ALU.mult))
        dr("V")
        P.emit("V", lambda v: v.tensor_tensor(
            slotti_sb[:, :], pi_sb[:, :], ji_sb[:, :], op=ALU.add))
        dr("V")
        P.emit("V", lambda v: v.tensor_tensor(
            slotti_sb[:, :], slotti_sb[:, :], vali_sb[:, :], op=ALU.mult))
        P.emit("V", lambda v: v.tensor_scalar(
            vali_sb[:, :], vali_sb[:, :], -4223, 4223,
            op0=ALU.mult, op1=ALU.add))
        dr("V")
        slot_ev = P.emit("V", lambda v: v.tensor_tensor(
            slotti_sb[:, :], slotti_sb[:, :], vali_sb[:, :], op=ALU.add),
            inc=("sV", 1))
        P.emit("G", lambda g, t=t: g.indirect_dma_start(
            out=bass.AP(d_xg, 0, [[E, 8 * CAP + 128], [1, E]]),
            out_offset=bass.IndirectOffsetOnAxis(sloti_sb[:, :], 0),
            in_=xn_sb[:, :], in_offset=None),
            waits=(("sV", slot_ev), ("sS", lnr[t]), ("dW", zf_ev)),
            inc=("dI", 16))
        P.emit("G", lambda g, t=t: g.indirect_dma_start(
            out=bass.AP(d_tidx, 0, [[1, 33 * 128], [1, 1]]),
            out_offset=bass.IndirectOffsetOnAxis(slotti_sb[:, :], 0),
            in_=tokids_sb[:, t:t + 1], in_offset=None), inc=("dI", 16))
        scat_ev = P.emit("G", lambda g: g.indirect_dma_start(
            out=bass.AP(d_gate, 0, [[1, 33 * 128], [1, 1]]),
            out_offset=bass.IndirectOffsetOnAxis(slotti_sb[:, :], 0),
            in_=g8_sb[:, :], in_offset=None), inc=("dI", 16))
        P.emit("S", None, waits=(("dI", scat_ev),))

    barrier()

    ev_w1 = [0] * NEXP
    ev_w2 = [0] * NEXP
    ev_w1h = [0] * NEXP
    ev_w2h = [0] * NEXP
    ev_b2 = [0] * NEXP
    ev_p1_lastuse = [0] * NEXP
    ev_p2_lastuse = [0] * NEXP

    def load_w(e):
        wt1 = (("sT", ev_p1_lastuse[e - 1]),) if e >= 1 else ()
        wt2 = (("sT", ev_p2_lastuse[e - 1]),) if e >= 1 else ()
        ev_w1h[e] = P.emit("G", lambda g, e=e: g.dma_start(
            out=w1_sb[:, :].rearrange("p (c f) -> p c f", c=8)[:, :, 0:FF // 2],
            in_=d_w1[e, :, 0:FF // 2].rearrange("(c p) f -> p c f", p=128)),
            waits=wt1, inc=("dW", 16))
        ev_w1[e] = P.emit("G", lambda g, e=e: g.dma_start(
            out=w1_sb[:, :].rearrange("p (c f) -> p c f", c=8)[:, :, FF // 2:],
            in_=d_w1[e, :, FF // 2:].rearrange("(c p) f -> p c f", p=128)),
            inc=("dW", 16))
        ev_w2h[e] = P.emit("G", lambda g, e=e: g.dma_start(
            out=w2_sb[:, :].rearrange("p (c f) -> p c f", c=32)[:, 0:16, :],
            in_=d_w2[e, 0:FF // 2].rearrange("(c p) f -> p c f", p=128)),
            waits=wt2, inc=("dW", 16))
        ev_w2[e] = P.emit("G", lambda g, e=e: g.dma_start(
            out=w2_sb[:, :].rearrange("p (c f) -> p c f", c=32)[:, 16:32, :],
            in_=d_w2[e, FF // 2:].rearrange("(c p) f -> p c f", p=128)),
            inc=("dW", 16))
        P.emit("G", lambda g, e=e: g.dma_start(
            out=b2b_sb[:, :], in_=d_b2b[e, :, :]),
            waits=(("sV", P.val("sV")),) if e >= 1 else (), inc=("dW", 16))
        ev_b2[e] = P.val("dW")

    gl = P.emit("Q", lambda q: q.dma_start(
        out=gate_sb[:, :], in_=bass.AP(d_gate, 0, [[32, 128], [1, 32]])),
        waits=(("dI", P.val("dI")),), inc=("dX", 16))
    tl = P.emit("Q", lambda q: q.dma_start(
        out=tidx_sb[:, :], in_=bass.AP(d_tidx, 0, [[32, 128], [1, 32]])),
        inc=("dX", 16))

    load_w(0)
    ev_xg_free = {}
    last_acc = 0
    for e in range(NEXP):
        p1last = None
        p2last = None
        for pr in range(2):
            xld = {}
            for h2 in range(2):
                gpr = e * 2 + pr
                st = pr * 2 + h2
                buf = (gpr % 2) * 2 + h2
                wtg = [("dX", tl)]
                prevfree = ev_xg_free.get((gpr - 2, h2))
                if prevfree:
                    wtg.append(("sT", prevfree))
                xld[h2] = P.emit("Q", lambda q, e=e, st=st, buf=buf: q.dma_start(
                    out=xg_sb[buf][:, :],
                    in_=bass.AP(d_xg, (e * CAP + st * 128) * E,
                                [[E, 128], [1, E]])),
                    waits=tuple(wtg), inc=("dX", 16))
            for h2 in range(2):
                gpr = e * 2 + pr
                buf = (gpr % 2) * 2 + h2
                P.emit("T", None, waits=(("dX", xld[h2]),))
                for fc in range(8):
                    po = psO[fc % 2]
                    tev = P.emit("T", lambda tt, buf=buf, fc=fc, po=po: tt.matmul(
                        po[:, :], xg_sb[buf][:, fc * 128:(fc + 1) * 128],
                        identb[:, :], start=True, stop=True),
                        waits=(("sS", max(0, P.val("sS") - 1)),),
                        inc=("sT", 1))
                    P.emit("S", lambda s, h2=h2, fc=fc, po=po: s.activation(
                        xgT[:, fc * 256 + h2 * 128: fc * 256 + h2 * 128 + 128],
                        po[:, :], AF.Identity),
                        waits=(("sT", tev),), inc=("sS", 1))
                ev_xg_free[(gpr, h2)] = tev
            xgT_done = P.val("sS")
            for fc in range(32):
                ps = psA if fc % 2 == 0 else psB
                mm = None
                for ech in range(8):
                    if ech == 0:
                        if fc == 0 and pr == 0:
                            wt = (("dW", ev_w1h[e]), ("sS", P.val("sS")))
                        elif fc == 16 and pr == 0:
                            wt = (("dW", ev_w1[e]), ("sS", max(0, P.val("sS") - 1)))
                        elif fc == 0:
                            wt = (("sS", P.val("sS")),)
                        else:
                            wt = (("sS", max(0, P.val("sS") - 1)),)
                    else:
                        wt = ()
                    mm = P.emit("T", lambda tt, ps=ps, e=e, fc=fc, ech=ech: tt.matmul(
                        ps[:, 0:256],
                        w1_sb[:, ech * FF + fc * 128: ech * FF + (fc + 1) * 128],
                        xgT[:, ech * 256:(ech + 1) * 256],
                        start=(ech == 0), stop=(ech == 7)),
                        waits=wt, inc=("sT", 1) if ech == 7 else None)
                p1last = mm
                P.emit("S", lambda s, ps=ps, e=e, fc=fc: s.activation(
                    hT_sb[:, fc * 256:(fc + 1) * 256], ps[:, 0:256], AF.Gelu,
                    bias=b1_sb[:, e * 32 + fc: e * 32 + fc + 1]),
                    waits=(("sT", mm),), inc=("sS", 1))
            hT_done = P.val("sS")
            for h2 in range(2):
                st = pr * 2 + h2
                psy = (psC, psD) if h2 == 0 else (psE, psF)
                mm = None
                for fc in range(32):
                    for half in range(2):
                        if fc == 0 and half == 0:
                            wt = (("sS", hT_done), ("dW", ev_w2h[e]),
                                  ("sV", P.val("sV")))
                        elif fc == 16 and half == 0:
                            wt = (("dW", ev_w2[e]),)
                        else:
                            wt = ()
                        mm = P.emit("T", lambda tt, psy=psy, fc=fc, half=half, h2=h2: tt.matmul(
                            psy[half][:, :],
                            hT_sb[:, fc * 256 + h2 * 128: fc * 256 + h2 * 128 + 128],
                            w2_sb[:, fc * E + half * 512: fc * E + (half + 1) * 512],
                            start=(fc == 0), stop=(fc == 31)),
                            waits=wt,
                            inc=("sT", 1) if (fc == 31 and half == 1) else None)
                if pr == 1 and h2 == 1:
                    p2last = mm
                yb = ybuf if st % 2 == 0 else ytmp2
                for half in range(2):
                    P.emit("V", lambda v, psy=psy, half=half, yb=yb: v.tensor_tensor(
                        yb[:, half * 512:(half + 1) * 512], psy[half][:, :],
                        b2b_sb[:, half * 512:(half + 1) * 512], op=ALU.add),
                        waits=(("sT", mm), ("dA", max(0, last_acc - 16)),
                               ("dW", ev_b2[e])) if half == 0 else ())
                dr("V")
                vev = P.emit("V", lambda v, e=e, st=st, yb=yb: v.tensor_scalar(
                    yb[:, :], yb[:, :],
                    gate_sb[:, e * 4 + st: e * 4 + st + 1],
                    None, op0=ALU.mult),
                    waits=(("dX", gl),), inc=("sV", 1))
                last_acc = P.emit("G", lambda g, e=e, st=st, yb=yb: g.indirect_dma_start(
                    out=d_y, out_offset=bass.IndirectOffsetOnAxis(
                        tidx_sb[:, e * 4 + st: e * 4 + st + 1], 0),
                    in_=yb[:, :], in_offset=None,
                    compute_op=ALU.add),
                    waits=(("sV", vev), ("dA", last_acc)), inc=("dA", 16))
        ev_p1_lastuse[e] = p1last
        ev_p2_lastuse[e] = p2last
        if e + 1 < NEXP:
            load_w(e + 1)

    P.emit("G", None, waits=(("dA", P.val("dA")), ("dI", P.val("dI"))))
    P.emit("G", lambda g: g.drain())

    with nc.Block() as block:
        semh = {}
        for s in SEMS:
            cm = nc.semaphore(s)
            semh[s] = cm.__enter__()
            cms.append(cm)

        def replay(eng, oplist):
            lastw = {}
            for waits, fn, inc in oplist:
                for s, v in waits:
                    if v > 0 and lastw.get(s, -1) < v:
                        eng.wait_ge(semh[s], v)
                        lastw[s] = v
                if fn is None:
                    continue
                inst = fn(eng)
                if inc is not None:
                    inst.then_inc(semh[inc[0]], inc[1])

        @block.tensor
        def _(t):
            replay(t, P.ops["T"])

        @block.vector
        def _(v):
            replay(v, P.ops["V"])

        @block.scalar
        def _(s):
            replay(s, P.ops["S"])

        @block.sync
        def _(q):
            replay(q, P.ops["Q"])

        @block.gpsimd
        def _(g):
            replay(g, P.ops["G"])

    return nc


def _host_prep(inp):
    X = np.ascontiguousarray(np.asarray(inp["L0_attn_X"], dtype=np.float32))
    Wqkv = np.asarray(inp["L0_attn_W_qkv"], dtype=np.float32)
    bqkv = np.asarray(inp["L0_attn_b_qkv"], dtype=np.float32)
    Wout = np.asarray(inp["L0_attn_W_out"], dtype=np.float32)
    bout = np.asarray(inp["L0_attn_b_out"], dtype=np.float32)
    W1 = np.asarray(inp["L0_ffn_W1"], dtype=np.float32)
    b1 = np.asarray(inp["L0_ffn_b1"], dtype=np.float32)
    W2 = np.asarray(inp["L0_ffn_W2"], dtype=np.float32)
    b2 = np.asarray(inp["L0_ffn_b2"], dtype=np.float32)
    Wr = np.asarray(inp["L0_ffn_Wr"], dtype=np.float32)
    br = np.asarray(inp["L0_ffn_br"], dtype=np.float32)

    Wsplit = Wqkv.reshape(E, H, 3, D)
    bsplit = bqkv.reshape(H, 3, D)
    shared = {
        "wq": np.ascontiguousarray(Wsplit[:, :, 0, :].reshape(E, E)),
        "wk": np.ascontiguousarray(Wsplit[:, :, 1, :].reshape(E, E)),
        "wv": np.ascontiguousarray(Wsplit[:, :, 2, :].reshape(E, E)),
        "wout": np.ascontiguousarray(Wout),
        "bqr": np.ascontiguousarray(bsplit[:, 0, :].reshape(E).reshape(8, 128).T),
        "bkr": np.ascontiguousarray(bsplit[:, 1, :].reshape(E).reshape(8, 128).T),
        "bvb": np.ascontiguousarray(
            np.broadcast_to(bsplit[:, 2, :].reshape(E), (128, E))),
        "boutb": np.ascontiguousarray(np.broadcast_to(bout, (128, E))),
        "w1": W1, "w2": W2,
        "b1r": np.ascontiguousarray(
            b1.reshape(NEXP, 32, 128).transpose(2, 0, 1).reshape(128, NEXP * 32)),
        "b2b": np.ascontiguousarray(
            np.broadcast_to(b2[:, None, :], (NEXP, 128, E))),
        "wr": np.ascontiguousarray(Wr),
        "brb": np.ascontiguousarray(np.broadcast_to(br, (128, NEXP))),
        "ident": np.eye(128, dtype=np.float32),
        "utri": np.triu(np.ones((128, 128), dtype=np.float32), 1),
        "onescol": np.ones((1, 128), dtype=np.float32),
        "onesrow": np.ones((128, 1), dtype=np.float32),
        "ecvals": np.ascontiguousarray(np.broadcast_to(
            (np.arange(NEXP, dtype=np.float32) * CAP), (128, NEXP))),
        "tokids": np.ascontiguousarray(
            (np.arange(NSLOT, dtype=np.int32)[None, :] * 128
             + np.arange(128, dtype=np.int32)[:, None])),
    }

    jj, ii = np.meshgrid(np.arange(512, dtype=np.float32),
                         np.arange(128, dtype=np.float32))
    bfull = (jj - ii).astype(np.float32)
    shared["bfull"] = bfull

    in_maps = []
    for c in range(8):
        b, p = c // 2, c % 2
        qts = 2 * np.arange(NSLOT) + p
        bdiag = np.empty((2, 128, 512), dtype=np.float32)
        for j in range(2):
            t = 256 * j + 128 * p
            bdiag[j] = np.where(bfull <= t, bfull, np.float32(MASKVAL))
        crep8 = np.zeros((H, NSLOT, 8), dtype=np.float32)
        for s in range(NSLOT):
            q0 = 128 * (2 * s + p)
            for ch in range(CNT[s]):
                crep8[:, s, ch] = 8.0 * _SLOPES * np.float32(512 * ch - q0)
        m = dict(shared)
        m["xq"] = np.ascontiguousarray(X[b].reshape(32, 128, E)[qts].reshape(NQ, E))
        m["xctx"] = np.ascontiguousarray(X[b])
        m["bdiag"] = bdiag
        m["crep8"] = np.ascontiguousarray(
            np.broadcast_to(crep8.reshape(1, -1), (128, H * NSLOT * 8)))
        in_maps.append(m)
    return in_maps


def kernel(**inp):
    in_maps = _host_prep(inp)
    nc = build_program()
    res = run_bass_kernel_spmd(nc, in_maps, core_ids=list(range(8)))
    out = np.empty((B, L, E), dtype=np.float32)
    for c in range(8):
        b, p = c // 2, c % 2
        qts = 2 * np.arange(NSLOT) + p
        y = np.asarray(res.results[c]["y"]).reshape(NSLOT, 128, E)
        out.reshape(B, 32, 128, E)[b, qts] = y
    return out


# version 16 (final)
# speedup vs baseline: 1.0263x; 1.0263x over previous
import numpy as np
import concourse.bass as bass
import concourse.mybir as mybir
from concourse.bass_utils import run_bass_kernel_spmd

F32 = mybir.dt.float32
BF16 = mybir.dt.bfloat16
AF = mybir.ActivationFunctionType
ALU = mybir.AluOpType
AX = mybir.AxisListType

B, L, E, H, D = 4, 4096, 1024, 16, 64
NQ = 2048
NSLOT = 16
CTX = 4096
NEXP, FF = 8, 4096
CNT = [(2 * s) // 4 + 1 for s in range(NSLOT)]
_WIN = [int(np.ceil(128.0 / float(s))) for s in
        (2.0 ** (-np.linspace(1.0, 8.0, H, dtype=np.float32).astype(np.float64))).astype(np.float32)]
CLO = [[max(0, (256 * s - _WIN[h]) // 512) for s in range(NSLOT)]
       for h in range(H)]
MASKVAL = -1.0e36
CAP = 512
TRASH = 8 * CAP
EPS = 1e-5

_SLOPES = (2.0 ** (-np.linspace(1.0, 8.0, H, dtype=np.float32).astype(np.float64))).astype(np.float32)

ENGS = ("T", "V", "S", "Q", "G")


class Prog:
    def __init__(self):
        self.ops = {e: [] for e in ENGS}
        self.cnt = {}

    def inc(self, sem, n=1):
        self.cnt[sem] = self.cnt.get(sem, 0) + n
        return self.cnt[sem]

    def val(self, sem):
        return self.cnt.get(sem, 0)

    def emit(self, eng, fn, waits=(), inc=None):
        waits = tuple(waits)
        v = None
        if inc is not None:
            if inc[0].startswith("d"):
                waits = waits + ((inc[0], self.val(inc[0])),)
            v = self.inc(inc[0], inc[1])
        self.ops[eng].append((waits, fn, inc))
        return v


def build_program():
    nc = bass.Bass("TRN2", target_bir_lowering=False)
    P = Prog()

    d_xq = nc.dram_tensor("xq", [NQ, E], F32, kind="ExternalInput").ap()
    d_xctx = nc.dram_tensor("xctx", [CTX, E], F32, kind="ExternalInput").ap()
    d_wq = nc.dram_tensor("wq", [E, E], F32, kind="ExternalInput").ap()
    d_wk = nc.dram_tensor("wk", [E, E], F32, kind="ExternalInput").ap()
    d_wv = nc.dram_tensor("wv", [E, E], F32, kind="ExternalInput").ap()
    d_wout = nc.dram_tensor("wout", [E, E], F32, kind="ExternalInput").ap()
    d_bqr = nc.dram_tensor("bqr", [128, 8], F32, kind="ExternalInput").ap()
    d_bkr = nc.dram_tensor("bkr", [128, 8], F32, kind="ExternalInput").ap()
    d_bvb = nc.dram_tensor("bvb", [128, E], F32, kind="ExternalInput").ap()
    d_boutb = nc.dram_tensor("boutb", [128, E], F32, kind="ExternalInput").ap()
    d_w1 = nc.dram_tensor("w1", [NEXP, E, FF], F32, kind="ExternalInput").ap()
    d_w2 = nc.dram_tensor("w2", [NEXP, FF, E], F32, kind="ExternalInput").ap()
    d_b1r = nc.dram_tensor("b1r", [128, NEXP * 32], F32, kind="ExternalInput").ap()
    d_b2b = nc.dram_tensor("b2b", [NEXP, 128, E], F32, kind="ExternalInput").ap()
    d_wr = nc.dram_tensor("wr", [E, NEXP], F32, kind="ExternalInput").ap()
    d_brb = nc.dram_tensor("brb", [128, NEXP], F32, kind="ExternalInput").ap()
    d_ident = nc.dram_tensor("ident", [128, 128], F32, kind="ExternalInput").ap()
    d_bfull = nc.dram_tensor("bfull", [128, 512], F32, kind="ExternalInput").ap()
    d_bdiag = nc.dram_tensor("bdiag", [2, 128, 512], F32, kind="ExternalInput").ap()
    d_crep8 = nc.dram_tensor("crep8", [128, H * NSLOT * 8], F32, kind="ExternalInput").ap()

    d_utri = nc.dram_tensor("utri", [128, 128], F32, kind="ExternalInput").ap()
    d_onescol = nc.dram_tensor("onescol", [1, 128], F32, kind="ExternalInput").ap()
    d_onesrow = nc.dram_tensor("onesrow", [128, 1], F32, kind="ExternalInput").ap()
    d_ecvals = nc.dram_tensor("ecvals", [128, NEXP], F32, kind="ExternalInput").ap()
    d_tokids = nc.dram_tensor("tokids", [128, NSLOT], mybir.dt.int32, kind="ExternalInput").ap()
    d_xg = nc.dram_tensor("xg", [8 * CAP + 128, E], BF16)
    d_gate = nc.dram_tensor("gatehbm", [33, 128], F32)
    d_tidx = nc.dram_tensor("tidxhbm", [33, 128], mybir.dt.int32)
    d_ksc = nc.dram_tensor("ksc", [H * D, CTX], BF16)
    d_vsc = nc.dram_tensor("vsc", [H, 32, 128, D], BF16)
    d_y = nc.dram_tensor("y", [NQ, E], F32, kind="ExternalOutput").ap()

    SEMS = ["sT", "sV", "sS", "sG", "dX", "dW", "dKV", "dKVL", "dO", "dA", "dI", "bar"]

    cms = []

    def sb(name, shape, dtype):
        cm = nc.sbuf_tensor(name, shape, dtype)
        h = cm.__enter__()
        cms.append(cm)
        return h

    def psum(name, shape):
        cm = nc.psum_tensor(name, shape, F32)
        h = cm.__enter__()
        cms.append(cm)
        return h

    identb = sb("identb", [128, 128], BF16)
    smb = sb("smb", [128, 16], F32)
    dump = sb("dump", [128, 1024], F32)
    xn_sb = sb("xn_sb", [128, 1024], BF16)
    xbuf = [sb(f"xbuf{i}", [128, E], F32) for i in range(3)]
    ybuf = sb("ybuf", [128, E], F32)
    bqr_sb = sb("bqr_sb", [128, 8], F32)
    bkr_sb = sb("bkr_sb", [128, 8], F32)
    wr_sb = sb("wr_sb", [128, 8 * NEXP], BF16)
    brb_sb = sb("brb_sb", [128, NEXP], F32)
    b1_sb = sb("b1_sb", [128, NEXP * 32], F32)
    s_all = sb("s_all", [128, NSLOT * NEXP], F32)
    logit_sb = sb("logit_sb", [128, NEXP], F32)
    e8_sb = sb("e8_sb", [128, NEXP], F32)
    g8_sb = sb("g8_sb", [128, 1], F32)
    m8_sb = sb("m8_sb", [128, 1], F32)
    ge_sb = sb("ge_sb", [128, 1], F32)
    sel_sb = sb("sel_sb", [128, 1], F32)
    ntk_sb = sb("ntk_sb", [128, 1], F32)
    mc_sb = [sb(f"mc{i}", [128, 8], F32) for i in range(2)]
    negm_sb = [sb(f"negm{i}", [128, 1], F32) for i in range(2)]
    biasb = [sb(f"biasb{i}", [128, 8], F32) for i in range(2)]
    lacc = [sb(f"lacc{i}", [128, 8], F32) for i in range(2)]
    lsum = [sb(f"lsum{i}", [128, 1], F32) for i in range(2)]
    rl_sb = [sb(f"rl{i}", [128, 1], F32) for i in range(2)]
    dump8 = sb("dump8", [128, 8], F32)
    utri_sb = sb("utri_sb", [128, 128], F32)
    onescol_sb = sb("onescol_sb", [1, 128], F32)
    onesrow_sb = sb("onesrow_sb", [128, 1], F32)
    ecvals_sb = sb("ecvals_sb", [128, NEXP], F32)
    tokids_sb = sb("tokids_sb", [128, NSLOT], mybir.dt.int32)
    oh_sb = sb("oh_sb", [128, NEXP], F32)
    totals_sb = sb("totals_sb", [1, NEXP], F32)
    tmp8_sb = sb("tmp8_sb", [128, NEXP], F32)
    posown_sb = sb("posown_sb", [128, 1], F32)
    ec_sb = sb("ec_sb", [128, 1], F32)
    val_sb = sb("val_sb", [128, 1], F32)
    valt_sb = sb("valt_sb", [128, 1], F32)
    slotf_sb = sb("slotf_sb", [128, 1], F32)
    sloti_sb = sb("sloti_sb", [128, 1], mybir.dt.int32)
    gate_sb = sb("gate_sb", [128, 32], F32)
    tidx_sb = sb("tidx_sb", [128, 32], mybir.dt.int32)
    ji_sb = sb("ji_sb", [128, 1], mybir.dt.int32)
    pi_sb = sb("pi_sb", [128, 1], mybir.dt.int32)
    slotti_sb = sb("slotti_sb", [128, 1], mybir.dt.int32)
    vali_sb = sb("vali_sb", [128, 1], mybir.dt.int32)

    SBUF_TOP = 16512 + 212863
    rem = nc.sbuf_bytes_remaining
    mb = SBUF_TOP - rem
    mb = (mb + 511) // 512 * 512
    need = 184320
    assert mb + need <= SBUF_TOP - 256, (mb, rem)

    def at(name, off, shape, dtype):
        return nc.alloc_sbuf_tensor_at(name, shape, dtype, offset=mb + off)

    wk_sb = at("wk_sb", 0, [128, 8 * E], BF16)
    wv_sb = at("wv_sb", 16384, [128, 8 * E], BF16)
    xnt_g = [at("xntg0", 32768, [128, 8 * 512], BF16),
             at("xntg1", 40960, [128, 8 * 512], BF16)]
    kstage = [at("kstage0", 49152, [128, 512], BF16),
              at("kstage1", 50176, [128, 512], BF16)]
    vstage = [at("vstage0", 51200, [128, 512], BF16),
              at("vstage1", 52224, [128, 512], BF16)]
    bvb_sb = at("bvb_sb", 53248, [128, E], F32)
    qT = at("qT", 0, [128, 8 * NQ], BF16)
    attnT = at("attnT", 32768, [128, 8 * NQ], BF16)
    wq_sb = at("wq_sb", 65536, [128, 8 * E], BF16)
    xn1qT = at("xn1qT", 81920, [128, 8 * NQ], BF16)
    kh_sb = at("kh_sb", 65536, [128, CTX], BF16)
    vh_sb = [at("vh0", 73728, [128, 32 * D], BF16),
             at("vh1", 77824, [128, 32 * D], BF16)]
    S_sb = [at("Ssb0", 81920, [128, CTX], F32),
            at("Ssb1", 98304, [128, CTX], F32)]
    Pt = [at("Pt0", 114688, [128, CTX], BF16),
          at("Pt1", 122880, [128, CTX], BF16),
          at("Pt2", 131072, [128, CTX], BF16)]
    PT = at("PT", 139264, [128, 32 * 128], BF16)
    crep_sb = at("crep_sb", 147456, [128, H * NSLOT * 8], F32)
    bfull_sb = at("bfull_sb", 155648, [128, 512], F32)
    bdiag_sb = at("bdiag_sb", 157696, [128, 2 * 512], F32)
    wout_sb = at("wout_sb", 65536, [128, 8 * E], BF16)
    boutb_sb = at("boutb_sb", 81920, [128, E], F32)
    resout2 = at("resout2", 98304, [128, E], F32)
    ytmp2 = at("ytmp2", 12288, [128, E], F32)
    xn2T = at("xn2T", 0, [128, 8 * NQ], BF16)
    xg_sb = [at("xg0", 0, [128, E], BF16), at("xg1", 2048, [128, E], BF16),
             at("xg2", 8192, [128, E], BF16), at("xg3", 10240, [128, E], BF16)]
    xgT = at("xgT", 4096, [128, 8 * 256], BF16)
    w1_sb = at("w1_sb", 32768, [128, 8 * FF], BF16)
    w2_sb = at("w2_sb", 98304, [128, 32 * E], BF16)
    hT_sb = at("hT_sb", 163840, [128, 32 * 256], BF16)
    b2b_sb = at("b2b_sb", 180224, [128, E], F32)

    psA = psum("psA", [128, 512])
    psB = psum("psB", [128, 512])
    psC = psum("psC", [128, 512])
    psD = psum("psD", [128, 512])
    psE = psum("psE", [128, 512])
    psF = psum("psF", [128, 512])
    psO = [psum(f"psO{i}", [128, 128]) for i in range(2)]


    def barrier():
        tgt = P.val("bar") + 5
        for e in ENGS:
            if e == "Q":
                P.emit(e, lambda eng: eng.nop(), inc=("bar", 1))
            else:
                P.emit(e, lambda eng: eng.drain(), inc=("bar", 1))
        for e in ENGS:
            P.emit(e, None, waits=(("bar", tgt),))

    def dr(eng):
        P.emit(eng, lambda x: x.drain())

    def ln_stats(src_ap, par):
        c = 8 * par
        s_sum = smb[:, c + 0:c + 1]
        s_sq = smb[:, c + 1:c + 2]
        s_mean = smb[:, c + 2:c + 3]
        s_v = smb[:, c + 3:c + 4]
        s_nmr = smb[:, c + 5:c + 6]
        vq = P.emit("S", lambda s: s.activation(dump[:, :], src_ap, AF.Square,
                                                accum_out=s_sq), inc=("sS", 1))
        P.emit("V", lambda v: v.reduce_sum(s_sum, src_ap, axis=AX.X))
        dr("V")
        P.emit("V", lambda v: v.tensor_scalar(s_mean, s_sum, 1.0 / E, None,
                                              op0=ALU.mult))
        P.emit("V", lambda v: v.tensor_scalar(s_v, s_sq, 1.0 / E, EPS,
                                              op0=ALU.mult, op1=ALU.add),
               waits=(("sS", vq),))
        dr("V")
        P.emit("V", lambda v: v.tensor_tensor(s_nmr, s_mean, s_mean, op=ALU.mult))
        dr("V")
        P.emit("V", lambda v: v.tensor_tensor(s_v, s_v, s_nmr, op=ALU.subtract))
        dr("V")
        P.emit("V", lambda v: v.reciprocal(s_v, s_v))
        return P.emit("V", lambda v: v.engine_nop(), inc=("sV", 1))

    def ln_finish(src_ap, dst_ap, par, vst):
        c = 8 * par
        s_mean = smb[:, c + 2:c + 3]
        s_v = smb[:, c + 3:c + 4]
        s_rstd = smb[:, c + 4:c + 5]
        s_nmr = smb[:, c + 5:c + 6]
        sq = P.emit("S", lambda s: s.activation(s_rstd, s_v, AF.Sqrt),
                    waits=(("sV", vst),), inc=("sS", 1))
        P.emit("V", lambda v: v.tensor_scalar(s_nmr, s_mean, s_rstd, -1.0,
                                              op0=ALU.mult, op1=ALU.mult),
               waits=(("sS", sq),))
        vfin = P.emit("V", lambda v: v.engine_nop(), inc=("sV", 1))
        dr("S")
        return P.emit("S", lambda s: s.activation(dst_ap, src_ap, AF.Identity,
                                                  bias=s_nmr, scale=s_rstd),
                      waits=(("sV", vfin),), inc=("sS", 1))

    def ln_tile(src_ap, dst_ap):
        return ln_finish(src_ap, dst_ap, 0, ln_stats(src_ap, 0))

    def transpose_1024(src_bf16, dst_tensor, dst_off, dst_row_stride):
        sev = None
        for half in range(2):
            ps = psC if half == 0 else psD
            tev = None
            for j in range(4):
                fc = half * 4 + j
                tev = P.emit("T", lambda tt, ps=ps, j=j, fc=fc, src=src_bf16: tt.matmul(
                    ps[:, j * 128:(j + 1) * 128],
                    src[:, fc * 128:(fc + 1) * 128], identb[:, :],
                    start=True, stop=True),
                    waits=(("sS", P.val("sS")),) if j == 0 else (),
                    inc=("sT", 1) if j == 3 else None)
            dst = bass.AP(dst_tensor, half * 4 * dst_row_stride + dst_off,
                          [[8 * dst_row_stride, 128], [dst_row_stride, 4], [1, 128]])
            sev = P.emit("S", lambda s, ps=ps, dst=dst: s.activation(
                dst, ps[:, :], AF.Identity), waits=(("sT", tev),), inc=("sS", 1))
        return sev

    def wload(dst, src):
        P.emit("G", lambda g, a=dst, b_=src: g.dma_start(
            out=a[:, :].rearrange("p (c f) -> p c f", c=8),
            in_=b_.rearrange("(c p) f -> p c f", p=128)), inc=("dW", 16))

    P.emit("G", lambda g: g.dma_start(out=identb[:, :], in_=d_ident), inc=("dW", 16))
    wload(wk_sb, d_wk)
    wload(wv_sb, d_wv)
    wload(wq_sb, d_wq)
    P.emit("G", lambda g: g.dma_start(
        out=wr_sb[:, :].rearrange("p (c f) -> p c f", c=8),
        in_=d_wr.rearrange("(c p) f -> p c f", p=128)), inc=("dW", 16))
    for ap_sb, ap_d in ((bqr_sb, d_bqr), (bkr_sb, d_bkr), (bvb_sb, d_bvb),
                        (brb_sb, d_brb), (bfull_sb, d_bfull),
                        (b1_sb, d_b1r), (crep_sb, d_crep8)):
        P.emit("Q", lambda q, a=ap_sb, b_=ap_d: q.dma_start(out=a[:, :], in_=b_),
               inc=("dX", 16))
    P.emit("Q", lambda q: q.dma_start(
        out=bdiag_sb[:, :].rearrange("p (s c) -> p s c", s=2),
        in_=d_bdiag.transpose([1, 0, 2])), inc=("dX", 16))
    for ap_sb, ap_d in ((utri_sb, d_utri), (onescol_sb, d_onescol),
                        (onesrow_sb, d_onesrow), (ecvals_sb, d_ecvals),
                        (tokids_sb, d_tokids)):
        P.emit("Q", lambda q, a=ap_sb, b_=ap_d: q.dma_start(out=a[:, :], in_=b_),
               inc=("dX", 16))
    w_const = P.val("dW")
    x_const = P.val("dX")
    P.emit("T", None, waits=(("dW", w_const), ("dX", x_const)))
    P.emit("V", None, waits=(("dX", x_const),))
    P.emit("S", None, waits=(("dX", x_const),))

    ln_done = {}
    ld = {}
    vst1 = {}
    xng_free = {}
    for t in range(3):
        ld[t] = P.emit("Q", lambda q, t=t, bs=t % 3: q.dma_start(
            out=xbuf[bs][:, :], in_=d_xctx[t * 128:(t + 1) * 128, :]),
            inc=("dX", 16))

    def s1_kv(g, xn):
        for hp in range(8):
            ps = psA if hp % 2 == 0 else psB
            mm = None
            for ech in range(8):
                mm = P.emit("T", lambda tt, ps=ps, ech=ech, hp=hp, xn=xn: tt.matmul(
                    ps[:, :], wk_sb[:, ech * E + hp * 128: ech * E + (hp + 1) * 128],
                    xn[:, ech * 512:(ech + 1) * 512],
                    start=(ech == 0), stop=(ech == 7)),
                    waits=(("sS", P.val("sS")),) if ech == 0 else (),
                    inc=("sT", 1) if ech == 7 else None)
            st = kstage[hp % 2]
            sev = P.emit("S", lambda s, ps=ps, st=st, hp=hp: s.activation(
                st[:, :], ps[:, :], AF.Identity, bias=bkr_sb[:, hp:hp + 1]),
                waits=(("sT", mm), ("dKV", max(0, P.val("dKV") - 16))),
                inc=("sS", 1))
            P.emit("Q", lambda q, st=st, hp=hp, g=g: q.dma_start(
                out=bass.AP(d_ksc, hp * 128 * CTX + g * 512,
                            [[CTX, 128], [1, 512]]),
                in_=st[:, :]), waits=(("sS", sev),), inc=("dKV", 16))
        for ti in range(4):
            for hh in range(2):
                ps = psE if hh == 0 else psF
                mm = None
                for ech in range(8):
                    mm = P.emit("T", lambda tt, ps=ps, ech=ech, hh=hh, ti=ti, xn=xn: tt.matmul(
                        ps[:, :], xn[:, ech * 512 + ti * 128: ech * 512 + (ti + 1) * 128],
                        wv_sb[:, ech * E + hh * 512: ech * E + (hh + 1) * 512],
                        start=(ech == 0), stop=(ech == 7)),
                        waits=(("sV", P.val("sV")),) if ech == 0 else (),
                        inc=("sT", 1) if ech == 7 else None)
                st = vstage[hh]
                vev = P.emit("V", lambda v, ps=ps, st=st, hh=hh: v.tensor_tensor(
                    st[:, :], ps[:, :], bvb_sb[:, hh * 512:(hh + 1) * 512],
                    op=ALU.add),
                    waits=(("sT", mm), ("dKV", max(0, P.val("dKV") - 16))),
                    inc=("sV", 1))
                P.emit("Q", lambda q, st=st, hh=hh, g=g, ti=ti: q.dma_start(
                    out=bass.AP(d_vsc, (hh * 8 * 32 + g * 4 + ti) * 128 * D,
                                [[D, 128], [32 * 128 * D, 8], [1, D]]),
                    in_=st[:, :].rearrange("p (h d) -> p h d", h=8)),
                    waits=(("sV", vev),), inc=("dKV", 16))

    def s1_finish(u):
        ev = ln_finish(xbuf[u % 3][:, :], xn_sb[:, :], u % 2, vst1[u])
        ln_done[u] = ev
        if u + 3 < 32:
            ld[u + 3] = P.emit("Q", lambda q, t2=u + 3, bs=(u + 3) % 3: q.dma_start(
                out=xbuf[bs][:, :], in_=d_xctx[t2 * 128:(t2 + 1) * 128, :]),
                waits=(("sS", ev),), inc=("dX", 16))
        g, ti = u // 4, u % 4
        P.emit("T", None, waits=(("sS", ev), xng_free[g]))
        transpose_1024(xn_sb, xnt_g[g % 2], ti * 128, 512)
        if ti == 3:
            s1_kv(g, xnt_g[g % 2])

    for t in range(32):
        if t % 4 == 0:
            xng_free[t // 4] = ("sT", P.val("sT"))
        P.emit("S", None, waits=(("dX", ld[t]),))
        P.emit("V", None, waits=(("dX", ld[t]),))
        if t == 1:
            dr("S")
        vst1[t] = ln_stats(xbuf[t % 3][:, :], t % 2)
        if t >= 1:
            s1_finish(t - 1)
    s1_finish(31)
    kv_done = ("dKV", P.val("dKV"))

    lnq = {}
    vstq = {}
    ldq = {}

    def s2a_finish(u):
        ev = ln_finish(xbuf[u % 3][:, :], xn_sb[:, :], u % 2, vstq[u])
        lnq[u] = ev
        if u + 3 < NSLOT:
            ldq[u + 3] = P.emit("Q", lambda q, t=u + 3, bs=(u + 3) % 3: q.dma_start(
                out=xbuf[bs][:, :], in_=d_xq[t * 128:(t + 1) * 128, :]),
                waits=(("sS", ev),), inc=("dX", 16))
        P.emit("T", None, waits=(("sS", ev),))
        transpose_1024(xn_sb, xn1qT, u * 128, NQ)

    for t in range(3):
        ldq[t] = P.emit("Q", lambda q, t=t, bs=t % 3: q.dma_start(
            out=xbuf[bs][:, :], in_=d_xq[t * 128:(t + 1) * 128, :]),
            waits=(("sS", ln_done[[30, 31, 29][t]]),), inc=("dX", 16))
    for t in range(NSLOT):
        P.emit("S", None, waits=(("dX", ldq[t]),))
        P.emit("V", None, waits=(("dX", ldq[t]),))
        if t == 1:
            dr("S")
        vstq[t] = ln_stats(xbuf[t % 3][:, :], t % 2)
        if t >= 1:
            s2a_finish(t - 1)
    s2a_finish(NSLOT - 1)
    for g in range(4):
        for hp in range(8):
            ps = psA if hp % 2 == 0 else psB
            mm = None
            for ech in range(8):
                mm = P.emit("T", lambda tt, ps=ps, ech=ech, hp=hp, g=g: tt.matmul(
                    ps[:, :], wq_sb[:, ech * E + hp * 128: ech * E + (hp + 1) * 128],
                    xn1qT[:, ech * NQ + g * 512: ech * NQ + (g + 1) * 512],
                    start=(ech == 0), stop=(ech == 7)),
                    waits=(("sS", P.val("sS")),) if ech == 0 else (),
                    inc=("sT", 1) if ech == 7 else None)
            P.emit("S", lambda s, ps=ps, hp=hp, g=g: s.activation(
                qT[:, hp * NQ + g * 512: hp * NQ + (g + 1) * 512],
                ps[:, :], AF.Identity, bias=bqr_sb[:, hp:hp + 1]),
                waits=(("sT", mm),), inc=("sS", 1))

    qproj_T = P.val("sT")
    qproj_S = P.val("sS")
    P.emit("T", None, waits=(("sS", qproj_S),))

    NIT = H * NSLOT

    evA_bias = [0] * NIT
    evB_exp = [0] * NIT
    evB_rl = [0] * NIT
    evB_norm = [0] * NIT
    evC_T = [0] * NIT
    evC_O = [0] * NIT
    ev_kvload = [0] * (H + 1)
    ev_headlastT = [0] * H

    def load_head(h):
        wt = []
        if h >= 2:
            wt.append(("sT", ev_headlastT[h - 2]))
        if h == 0:
            wt.append(kv_done)
            wt.append(("sT", qproj_T))
        P.emit("Q", lambda q, h=h: q.dma_start(
            out=kh_sb[(h % 2) * 64:(h % 2) * 64 + 64, :],
            in_=bass.AP(d_ksc, h * 64 * CTX, [[CTX, 64], [1, CTX]])),
            waits=tuple(wt), inc=("dKVL", 16))
        P.emit("Q", lambda q, h=h: q.dma_start(
            out=vh_sb[h % 2][:, :].rearrange("p (t d) -> p t d", t=32),
            in_=bass.AP(d_vsc, h * 32 * 128 * D,
                        [[D, 128], [128 * D, 32], [1, D]])),
            inc=("dKVL", 16))
        ev_kvload[h] = P.val("dKVL")

    load_head(0)
    load_head(1)

    gchunk = [0]
    chunk_stt_ev = {}
    gtg = [0]
    tg_evac_ev = {}

    def stageA(i):
        h, s = divmod(i, NSLOT)
        cnt = CNT[s]
        hp, hb = h // 2, (h % 2) * 64
        par = i % 2
        if s == 3 and h + 1 < H:
            load_head(h + 1)
        cbase = (h * NSLOT + s) * 8
        c_lo = CLO[h][s]
        for c in range(c_lo, cnt):
            gc = gchunk[0]
            ps = (psA, psB, psE, psF)[gc % 4]
            wt = [("sV", chunk_stt_ev.get(gc - 4, 0))]
            if c == c_lo:
                wt.append(("dKVL", ev_kvload[h]))
                if i >= 2:
                    wt.append(("sS", evB_exp[i - 2]))
            P.emit("T", lambda tt, ps=ps, c=c, hp=hp, hb=hb, s=s: tt.matmul(
                ps[:, :],
                qT[hb:hb + 64, hp * NQ + s * 128: hp * NQ + (s + 1) * 128],
                kh_sb[hb:hb + 64, c * 512:(c + 1) * 512],
                start=True, stop=True),
                waits=tuple(wt), inc=("sT", 1))
            tval = P.val("sT")
            btile = (bfull_sb[:, :] if c < cnt - 1
                     else bdiag_sb[:, (s % 2) * 512:(s % 2 + 1) * 512])
            sc = float(8.0 * _SLOPES[h])
            P.emit("V", lambda v, b=btile, sc=sc, par=par, c=c, ps=ps: v.scalar_tensor_tensor(
                S_sb[par][:, c * 512:(c + 1) * 512], b, sc, ps[:, :],
                op0=ALU.mult, op1=ALU.add),
                waits=(("sT", tval),))
            chunk_stt_ev[gc] = P.emit("V", lambda v: v.engine_nop(), inc=("sV", 1))
            dr("V")
            P.emit("V", lambda v, par=par, c=c: v.reduce_max(
                mc_sb[par][:, c:c + 1], S_sb[par][:, c * 512:(c + 1) * 512],
                axis=AX.X))
            gchunk[0] += 1
        dr("V")
        P.emit("V", lambda v, par=par, cnt=cnt, cbase=cbase, c_lo=c_lo: v.tensor_tensor(
            mc_sb[par][:, c_lo:cnt], mc_sb[par][:, c_lo:cnt],
            crep_sb[:, cbase + c_lo:cbase + cnt], op=ALU.add))
        dr("V")
        P.emit("V", lambda v, par=par, cnt=cnt, c_lo=c_lo: v.reduce_max(
            negm_sb[par][:, :], mc_sb[par][:, c_lo:cnt], axis=AX.X, negate=True))
        dr("V")
        wtb = (("sS", evB_exp[i - 2]),) if i >= 2 else ()
        for c in range(c_lo, cnt):
            P.emit("V", lambda v, par=par, c=c, cbase=cbase: v.tensor_scalar(
                biasb[par][:, c:c + 1], negm_sb[par][:, :],
                crep_sb[:, cbase + c:cbase + c + 1], 0.125,
                op0=ALU.add, op1=ALU.mult),
                waits=wtb if c == c_lo else ())
        evA_bias[i] = P.emit("V", lambda v: v.engine_nop(), inc=("sV", 1))

    def stageB1(i):
        h, s = divmod(i, NSLOT)
        cnt = CNT[s]
        c_lo = CLO[h][s]
        par = i % 2
        p3 = i % 3
        wt = [("sV", evA_bias[i])]
        if i >= 3:
            wt.append(("sT", evC_T[i - 3]))
        for c in range(c_lo, cnt):
            P.emit("S", lambda s_, par=par, p3=p3, c=c: s_.activation(
                Pt[p3][:, c * 512:(c + 1) * 512],
                S_sb[par][:, c * 512:(c + 1) * 512],
                AF.Exp, bias=biasb[par][:, c:c + 1], scale=0.125,
                accum_out=lacc[par][:, c:c + 1]),
                waits=tuple(wt) if c == c_lo else ())
        wl = (("sV", evB_rl[i - 2]),) if i >= 2 else ()
        dr("S")
        evB_exp[i] = P.emit("S", lambda s_, par=par, cnt=cnt, c_lo=c_lo: s_.activation(
            dump8[:, c_lo:cnt], lacc[par][:, c_lo:cnt], AF.Identity,
            accum_out=lsum[par][:, :]), waits=wl, inc=("sS", 1))

    def stageB2(i):
        par = i % 2
        p3 = i % 3
        wt = [("sS", evB_exp[i])]
        evB_rl[i] = P.emit("V", lambda v, par=par: v.reciprocal(
            rl_sb[par][:, :], lsum[par][:, :]), waits=tuple(wt), inc=("sV", 1))
        h, s = divmod(i, NSLOT)
        lo = CLO[h][s] * 512
        ctx = CNT[s] * 512
        dr("V")
        evB_norm[i] = P.emit("V", lambda v, p3=p3, par=par, lo=lo, ctx=ctx: v.tensor_scalar(
            Pt[p3][:, lo:ctx], Pt[p3][:, lo:ctx], rl_sb[par][:, :], None,
            op0=ALU.mult), inc=("sV", 1))

    def stageC(i):
        h, s = divmod(i, NSLOT)
        cnt = CNT[s]
        c_lo = CLO[h][s]
        nks = cnt * 4
        p3 = i % 3
        opar = i % 2
        hb = (h % 2) * 64
        hp = h // 2
        evac_of_tg = {}

        av_last = [0]

        def emit_av(tg):
            wtav = [("sS", evac_of_tg[tg])]
            if tg == c_lo and i >= 2:
                wtav.append(("sS", evC_O[i - 2]))
            for j in range(4):
                ks = tg * 4 + j
                last = (ks == nks - 1)
                ev = P.emit("T", lambda tt, ks=ks, hb=hb, opar=opar, h=h,
                       first=(tg == c_lo and j == 0),
                       last=last: tt.matmul(
                    psO[opar][hb:hb + 64, :],
                    vh_sb[h % 2][:, ks * 64:(ks + 1) * 64],
                    PT[:, ks * 128:(ks + 1) * 128],
                    start=first, stop=last, tile_position=(0, hb)),
                    waits=tuple(wtav) if j == 0 else (),
                    inc=("sT", 1) if last else None)
                if last:
                    av_last[0] = ev

        for tg in range(c_lo, cnt):
            gt = gtg[0]
            ps = psC if gt % 2 == 0 else psD
            wt = [("sV", evB_norm[i]), ("sS", tg_evac_ev.get(gt - 2, 0))]
            trev = None
            for j in range(4):
                ks = tg * 4 + j
                trev = P.emit("T", lambda tt, ps=ps, j=j, ks=ks, p3=p3: tt.matmul(
                    ps[:, j * 128:(j + 1) * 128],
                    Pt[p3][:, ks * 128:(ks + 1) * 128], identb[:, :],
                    start=True, stop=True),
                    waits=tuple(wt) if j == 0 else (),
                    inc=("sT", 1) if j == 3 else None)
            sev = P.emit("S", lambda s_, ps=ps, tg=tg: s_.activation(
                PT[:, tg * 512:(tg + 1) * 512], ps[:, :], AF.Identity),
                waits=(("sT", trev),), inc=("sS", 1))
            tg_evac_ev[gt] = sev
            evac_of_tg[tg] = sev
            gtg[0] += 1
            if tg >= c_lo + 1:
                emit_av(tg - 1)
        emit_av(cnt - 1)
        evC_T[i] = av_last[0]
        if s == NSLOT - 1:
            ev_headlastT[h] = evC_T[i]
        evC_O[i] = P.emit("S", lambda s_, hb=hb, opar=opar, hp=hp, s=s: s_.activation(
            attnT[hb:hb + 64, hp * NQ + s * 128: hp * NQ + (s + 1) * 128],
            psO[opar][hb:hb + 64, 0:128], AF.Identity),
            waits=(("sT", evC_T[i]),), inc=("sS", 1))

    for r in range(NIT + 3):
        if r < NIT:
            stageA(r)
        if 1 <= r <= NIT:
            stageB1(r - 1)
        if 2 <= r <= NIT + 1:
            stageB2(r - 2)
        if 3 <= r <= NIT + 2:
            stageC(r - 3)

    attn_T = P.val("sT")
    attn_S = P.val("sS")
    attn_V = P.val("sV")

    out_ev = {}
    P.emit("G", lambda g: g.dma_start(
        out=wout_sb[:, :].rearrange("p (c f) -> p c f", c=8),
        in_=d_wout.rearrange("(c p) f -> p c f", p=128)),
        waits=(("sT", attn_T),), inc=("dW", 16))
    P.emit("Q", lambda q: q.dma_start(out=boutb_sb[:, :], in_=d_boutb),
           waits=(("sV", attn_V),), inc=("dX", 16))
    wout_ld = P.val("dW")
    boutb_ld = P.val("dX")
    for t in range(NSLOT):
        lev = P.emit("Q", lambda q, t=t, bs=t % 3: q.dma_start(
            out=xbuf[bs][:, :], in_=d_xq[t * 128:(t + 1) * 128, :]),
            waits=(("sV", P.val("sV")),) if t >= 3 else (), inc=("dX", 16))
        for half in range(2):
            ps = psE if half == 0 else psF
            mm = None
            for fc in range(8):
                wt = ()
                if fc == 0:
                    wt = (("sV", P.val("sV")), ("dW", wout_ld),
                          ("sS", attn_S))
                mm = P.emit("T", lambda tt, ps=ps, fc=fc, t=t, half=half: tt.matmul(
                    ps[:, :],
                    attnT[:, fc * NQ + t * 128: fc * NQ + (t + 1) * 128],
                    wout_sb[:, fc * E + half * 512: fc * E + (half + 1) * 512],
                    start=(fc == 0), stop=(fc == 7)),
                    waits=wt, inc=("sT", 1) if fc == 7 else None)
            P.emit("V", lambda v, ps=ps, half=half: v.tensor_tensor(
                dump[:, half * 512:(half + 1) * 512], ps[:, :],
                boutb_sb[:, half * 512:(half + 1) * 512], op=ALU.add),
                waits=(("sT", mm), ("dX", boutb_ld)))
        dr("V")
        ro = ybuf if t % 2 == 0 else resout2
        vev = P.emit("V", lambda v, bs=t % 3, ro=ro: v.tensor_tensor(
            ro[:, :], dump[:, :], xbuf[bs][:, :], op=ALU.add),
            waits=(("dX", lev), ("dO", max(0, P.val("dO") - 16))), inc=("sV", 1))
        out_ev[t] = P.emit("Q", lambda q, t=t, ro=ro: q.dma_start(
            out=d_y[t * 128:(t + 1) * 128, :], in_=ro[:, :]),
            waits=(("sV", vev),), inc=("dO", 16))
    out_written = ("dO", P.val("dO"))

    zm = P.emit("V", lambda v: v.memset(ybuf[:, :], 0.0),
                waits=(out_written,))
    zm = P.emit("V", lambda v: v.drain(), inc=("sV", 1))
    for j in range(33):
        P.emit("G", lambda g, j=j: g.dma_start(
            out=bass.AP(d_xg, j * 128 * E, [[E, 128], [1, E]]),
            in_=ybuf[:, :].bitcast(BF16)[:, 0:E]),
            waits=(("sV", zm),) if j == 0 else (), inc=("dW", 16))
    P.emit("G", lambda g: g.dma_start(
        out=bass.AP(d_gate, 0, [[128, 33], [1, 128]]),
        in_=ybuf[0:33, 0:128]), inc=("dW", 16))
    zf_ev = P.emit("G", lambda g: g.dma_start(
        out=bass.AP(d_tidx, 0, [[128, 33], [1, 128]]),
        in_=ybuf[:, :].bitcast(mybir.dt.int32)[0:33, 0:128]), inc=("dW", 16))
    s2c_V = P.val("sV")
    lnr = {}
    vstr = {}
    tot_ready = None
    scat_ev = None

    def s3a_consume(t):
        nonlocal tot_ready, scat_ev
        ev = ln_finish(xbuf[t % 3][:, :], xn_sb[:, :], t % 2, vstr[t])
        lnr[t] = ev
        P.emit("T", None, waits=(("sS", ev),))
        transpose_1024(xn_sb, xn2T, t * 128, NQ)
        mm = None
        for ech in range(8):
            mm = P.emit("T", lambda tt, ech=ech, t=t: tt.matmul(
                psO[0][:, 0:NEXP],
                xn2T[:, ech * NQ + t * 128: ech * NQ + (t + 1) * 128],
                wr_sb[:, ech * NEXP:(ech + 1) * NEXP],
                start=(ech == 0), stop=(ech == 7)),
                waits=(("sS", P.val("sS")), ("sV", P.val("sV"))) if ech == 0 else (),
                inc=("sT", 1) if ech == 7 else None)
        P.emit("V", lambda v: v.tensor_tensor(
            logit_sb[:, :], psO[0][:, 0:NEXP], brb_sb[:, :], op=ALU.add),
            waits=(("sT", mm), ("dI", P.val("dI"))))
        dr("V")
        P.emit("V", lambda v: v.reduce_max(m8_sb[:, :], logit_sb[:, :],
                                           axis=AX.X, negate=True))
        vv = P.emit("V", lambda v: v.engine_nop(), inc=("sV", 1))
        sv = P.emit("S", lambda s: s.activation(
            e8_sb[:, :], logit_sb[:, :], AF.Exp, bias=m8_sb[:, :],
            accum_out=g8_sb[:, :]), waits=(("sV", vv),), inc=("sS", 1))
        P.emit("V", lambda v: v.reciprocal(g8_sb[:, :], g8_sb[:, :]),
               waits=(("sS", sv),))
        P.emit("V", lambda v: v.tensor_scalar(
            oh_sb[:, :], logit_sb[:, :], m8_sb[:, :], None, op0=ALU.add))
        dr("V")
        P.emit("V", lambda v: v.tensor_scalar(
            oh_sb[:, :], oh_sb[:, :], 0.0, None, op0=ALU.is_ge))
        dr("V")
        oh_done = P.emit("V", lambda v: v.engine_nop(), inc=("sV", 1))
        if t == 0:
            P.emit("V", lambda v: v.memset(totals_sb[:, :], 0.0))
            tot_ready = P.emit("V", lambda v: v.drain(), inc=("sV", 1))
        P.emit("T", lambda tt: tt.matmul(
            psO[1][:, 0:NEXP], utri_sb[:, :], oh_sb[:, :],
            start=True, stop=False),
            waits=(("sV", oh_done), ("sV", P.val("sV"))))
        posmm = P.emit("T", lambda tt: tt.matmul(
            psO[1][:, 0:NEXP], onescol_sb[:, :], totals_sb[:, :],
            start=False, stop=True),
            waits=(("sV", tot_ready),), inc=("sT", 1))
        colsum_ev = P.emit("T", lambda tt: tt.matmul(
            psE[0:1, 0:NEXP], onesrow_sb[:, :], oh_sb[:, :],
            start=True, stop=True),
            waits=(("sV", tot_ready),), inc=("sT", 1))
        tot_ready = P.emit("V", lambda v: v.tensor_tensor(
            totals_sb[:, :], totals_sb[:, :], psE[0:1, 0:NEXP], op=ALU.add),
            waits=(("sT", colsum_ev),), inc=("sV", 1))
        wsl = [("sT", posmm)]
        if t >= 1:
            wsl.append(("dI", scat_ev))
        P.emit("V", lambda v: v.tensor_tensor(
            tmp8_sb[:, :], psO[1][:, 0:NEXP], oh_sb[:, :], op=ALU.mult),
            waits=tuple(wsl))
        dr("V")
        P.emit("V", lambda v: v.reduce_sum(posown_sb[:, :], tmp8_sb[:, :],
                                           axis=AX.X))
        dr("V")
        P.emit("V", lambda v: v.tensor_tensor(
            tmp8_sb[:, :], oh_sb[:, :], ecvals_sb[:, :], op=ALU.mult))
        dr("V")
        P.emit("V", lambda v: v.reduce_sum(ec_sb[:, :], tmp8_sb[:, :],
                                           axis=AX.X))
        dr("V")
        P.emit("V", lambda v: v.tensor_scalar(
            val_sb[:, :], posown_sb[:, :], float(CAP), None, op0=ALU.is_lt))
        P.emit("V", lambda v: v.tensor_tensor(
            slotf_sb[:, :], posown_sb[:, :], ec_sb[:, :], op=ALU.add))
        dr("V")
        P.emit("V", lambda v: v.tensor_scalar(
            valt_sb[:, :], val_sb[:, :], -float(TRASH), float(TRASH),
            op0=ALU.mult, op1=ALU.add))
        P.emit("V", lambda v: v.tensor_tensor(
            slotf_sb[:, :], slotf_sb[:, :], val_sb[:, :], op=ALU.mult))
        dr("V")
        P.emit("V", lambda v: v.tensor_tensor(
            slotf_sb[:, :], slotf_sb[:, :], valt_sb[:, :], op=ALU.add))
        dr("V")
        P.emit("V", lambda v: v.tensor_copy(
            sloti_sb[:, :], slotf_sb[:, :]))
        P.emit("V", lambda v: v.tensor_copy(vali_sb[:, :], val_sb[:, :]))
        dr("V")
        P.emit("V", lambda v: v.tensor_scalar(
            ji_sb[:, :], sloti_sb[:, :], 7, None, op0=ALU.arith_shift_right))
        P.emit("V", lambda v: v.tensor_scalar(
            pi_sb[:, :], sloti_sb[:, :], 127, None, op0=ALU.bitwise_and))
        dr("V")
        P.emit("V", lambda v: v.tensor_scalar(
            pi_sb[:, :], pi_sb[:, :], 32, None, op0=ALU.mult))
        dr("V")
        P.emit("V", lambda v: v.tensor_tensor(
            slotti_sb[:, :], pi_sb[:, :], ji_sb[:, :], op=ALU.add))
        dr("V")
        P.emit("V", lambda v: v.tensor_tensor(
            slotti_sb[:, :], slotti_sb[:, :], vali_sb[:, :], op=ALU.mult))
        P.emit("V", lambda v: v.tensor_scalar(
            vali_sb[:, :], vali_sb[:, :], -4223, 4223,
            op0=ALU.mult, op1=ALU.add))
        dr("V")
        slot_ev = P.emit("V", lambda v: v.tensor_tensor(
            slotti_sb[:, :], slotti_sb[:, :], vali_sb[:, :], op=ALU.add),
            inc=("sV", 1))
        P.emit("G", lambda g, t=t: g.indirect_dma_start(
            out=bass.AP(d_xg, 0, [[E, 8 * CAP + 128], [1, E]]),
            out_offset=bass.IndirectOffsetOnAxis(sloti_sb[:, :], 0),
            in_=xn_sb[:, :], in_offset=None),
            waits=(("sV", slot_ev), ("sS", lnr[t]), ("dW", zf_ev)),
            inc=("dI", 16))
        P.emit("G", lambda g, t=t: g.indirect_dma_start(
            out=bass.AP(d_tidx, 0, [[1, 33 * 128], [1, 1]]),
            out_offset=bass.IndirectOffsetOnAxis(slotti_sb[:, :], 0),
            in_=tokids_sb[:, t:t + 1], in_offset=None), inc=("dI", 16))
        scat_ev = P.emit("G", lambda g: g.indirect_dma_start(
            out=bass.AP(d_gate, 0, [[1, 33 * 128], [1, 1]]),
            out_offset=bass.IndirectOffsetOnAxis(slotti_sb[:, :], 0),
            in_=g8_sb[:, :], in_offset=None), inc=("dI", 16))
        P.emit("S", None, waits=(("dI", scat_ev),))

    for t in range(NSLOT):
        wt = [("sS", lnr[t - 3])] if t >= 3 else [("sV", s2c_V)]
        wt.append(("dO", out_ev[t]))
        lev = P.emit("Q", lambda q, t=t, bs=t % 3: q.dma_start(
            out=xbuf[bs][:, :], in_=d_y[t * 128:(t + 1) * 128, :]),
            waits=tuple(wt), inc=("dX", 16))
        P.emit("S", None, waits=(("dX", lev),) + ((("sV", s2c_V),) if t == 0 else ()))
        P.emit("V", None, waits=(("dX", lev),))
        if t == 1:
            dr("S")
        vstr[t] = ln_stats(xbuf[t % 3][:, :], t % 2)
        if t >= 1:
            s3a_consume(t - 1)
    s3a_consume(NSLOT - 1)

    barrier()

    ev_w1 = [0] * NEXP
    ev_w2 = [0] * NEXP
    ev_w1h = [0] * NEXP
    ev_w2h = [0] * NEXP
    ev_b2 = [0] * NEXP
    ev_p1_lastuse = [0] * NEXP
    ev_p2_lastuse = [0] * NEXP

    def load_w(e):
        wt1 = (("sT", ev_p1_lastuse[e - 1]),) if e >= 1 else ()
        wt2 = (("sT", ev_p2_lastuse[e - 1]),) if e >= 1 else ()
        ev_w1h[e] = P.emit("G", lambda g, e=e: g.dma_start(
            out=w1_sb[:, :].rearrange("p (c f) -> p c f", c=8)[:, :, 0:FF // 2],
            in_=d_w1[e, :, 0:FF // 2].rearrange("(c p) f -> p c f", p=128)),
            waits=wt1, inc=("dW", 16))
        ev_w1[e] = P.emit("G", lambda g, e=e: g.dma_start(
            out=w1_sb[:, :].rearrange("p (c f) -> p c f", c=8)[:, :, FF // 2:],
            in_=d_w1[e, :, FF // 2:].rearrange("(c p) f -> p c f", p=128)),
            inc=("dW", 16))
        ev_w2h[e] = P.emit("G", lambda g, e=e: g.dma_start(
            out=w2_sb[:, :].rearrange("p (c f) -> p c f", c=32)[:, 0:16, :],
            in_=d_w2[e, 0:FF // 2].rearrange("(c p) f -> p c f", p=128)),
            waits=wt2, inc=("dW", 16))
        ev_w2[e] = P.emit("G", lambda g, e=e: g.dma_start(
            out=w2_sb[:, :].rearrange("p (c f) -> p c f", c=32)[:, 16:32, :],
            in_=d_w2[e, FF // 2:].rearrange("(c p) f -> p c f", p=128)),
            inc=("dW", 16))
        P.emit("G", lambda g, e=e: g.dma_start(
            out=b2b_sb[:, :], in_=d_b2b[e, :, :]),
            waits=(("sV", P.val("sV")),) if e >= 1 else (), inc=("dW", 16))
        ev_b2[e] = P.val("dW")

    gl = P.emit("Q", lambda q: q.dma_start(
        out=gate_sb[:, :], in_=bass.AP(d_gate, 0, [[32, 128], [1, 32]])),
        waits=(("dI", P.val("dI")),), inc=("dX", 16))
    tl = P.emit("Q", lambda q: q.dma_start(
        out=tidx_sb[:, :], in_=bass.AP(d_tidx, 0, [[32, 128], [1, 32]])),
        inc=("dX", 16))

    load_w(0)
    ev_xg_free = {}
    last_acc = 0
    for e in range(NEXP):
        p1last = None
        p2last = None
        for pr in range(2):
            xld = {}
            for h2 in range(2):
                gpr = e * 2 + pr
                st = pr * 2 + h2
                buf = (gpr % 2) * 2 + h2
                wtg = [("dX", tl)]
                prevfree = ev_xg_free.get((gpr - 2, h2))
                if prevfree:
                    wtg.append(("sT", prevfree))
                xld[h2] = P.emit("Q", lambda q, e=e, st=st, buf=buf: q.dma_start(
                    out=xg_sb[buf][:, :],
                    in_=bass.AP(d_xg, (e * CAP + st * 128) * E,
                                [[E, 128], [1, E]])),
                    waits=tuple(wtg), inc=("dX", 16))
            for h2 in range(2):
                gpr = e * 2 + pr
                buf = (gpr % 2) * 2 + h2
                P.emit("T", None, waits=(("dX", xld[h2]),))
                for fc in range(8):
                    po = psO[fc % 2]
                    tev = P.emit("T", lambda tt, buf=buf, fc=fc, po=po: tt.matmul(
                        po[:, :], xg_sb[buf][:, fc * 128:(fc + 1) * 128],
                        identb[:, :], start=True, stop=True),
                        waits=(("sS", max(0, P.val("sS") - 1)),),
                        inc=("sT", 1))
                    P.emit("S", lambda s, h2=h2, fc=fc, po=po: s.activation(
                        xgT[:, fc * 256 + h2 * 128: fc * 256 + h2 * 128 + 128],
                        po[:, :], AF.Identity),
                        waits=(("sT", tev),), inc=("sS", 1))
                ev_xg_free[(gpr, h2)] = tev
            xgT_done = P.val("sS")
            for fc in range(32):
                ps = psA if fc % 2 == 0 else psB
                mm = None
                for ech in range(8):
                    if ech == 0:
                        if fc == 0 and pr == 0:
                            wt = (("dW", ev_w1h[e]), ("sS", P.val("sS")))
                        elif fc == 16 and pr == 0:
                            wt = (("dW", ev_w1[e]), ("sS", max(0, P.val("sS") - 1)))
                        elif fc == 0:
                            wt = (("sS", P.val("sS")),)
                        else:
                            wt = (("sS", max(0, P.val("sS") - 1)),)
                    else:
                        wt = ()
                    mm = P.emit("T", lambda tt, ps=ps, e=e, fc=fc, ech=ech: tt.matmul(
                        ps[:, 0:256],
                        w1_sb[:, ech * FF + fc * 128: ech * FF + (fc + 1) * 128],
                        xgT[:, ech * 256:(ech + 1) * 256],
                        start=(ech == 0), stop=(ech == 7)),
                        waits=wt, inc=("sT", 1) if ech == 7 else None)
                p1last = mm
                P.emit("S", lambda s, ps=ps, e=e, fc=fc: s.activation(
                    hT_sb[:, fc * 256:(fc + 1) * 256], ps[:, 0:256], AF.Gelu,
                    bias=b1_sb[:, e * 32 + fc: e * 32 + fc + 1]),
                    waits=(("sT", mm),), inc=("sS", 1))
            hT_done = P.val("sS")
            for h2 in range(2):
                st = pr * 2 + h2
                psy = (psC, psD) if h2 == 0 else (psE, psF)
                mm = None
                for fc in range(32):
                    for half in range(2):
                        if fc == 0 and half == 0:
                            wt = (("sS", hT_done), ("dW", ev_w2h[e]),
                                  ("sV", P.val("sV")))
                        elif fc == 16 and half == 0:
                            wt = (("dW", ev_w2[e]),)
                        else:
                            wt = ()
                        mm = P.emit("T", lambda tt, psy=psy, fc=fc, half=half, h2=h2: tt.matmul(
                            psy[half][:, :],
                            hT_sb[:, fc * 256 + h2 * 128: fc * 256 + h2 * 128 + 128],
                            w2_sb[:, fc * E + half * 512: fc * E + (half + 1) * 512],
                            start=(fc == 0), stop=(fc == 31)),
                            waits=wt,
                            inc=("sT", 1) if (fc == 31 and half == 1) else None)
                if pr == 1 and h2 == 1:
                    p2last = mm
                yb = ybuf if st % 2 == 0 else ytmp2
                for half in range(2):
                    P.emit("V", lambda v, psy=psy, half=half, yb=yb: v.tensor_tensor(
                        yb[:, half * 512:(half + 1) * 512], psy[half][:, :],
                        b2b_sb[:, half * 512:(half + 1) * 512], op=ALU.add),
                        waits=(("sT", mm), ("dA", max(0, last_acc - 16)),
                               ("dW", ev_b2[e])) if half == 0 else ())
                dr("V")
                vev = P.emit("V", lambda v, e=e, st=st, yb=yb: v.tensor_scalar(
                    yb[:, :], yb[:, :],
                    gate_sb[:, e * 4 + st: e * 4 + st + 1],
                    None, op0=ALU.mult),
                    waits=(("dX", gl),), inc=("sV", 1))
                last_acc = P.emit("G", lambda g, e=e, st=st, yb=yb: g.indirect_dma_start(
                    out=d_y, out_offset=bass.IndirectOffsetOnAxis(
                        tidx_sb[:, e * 4 + st: e * 4 + st + 1], 0),
                    in_=yb[:, :], in_offset=None,
                    compute_op=ALU.add),
                    waits=(("sV", vev), ("dA", last_acc)), inc=("dA", 16))
        ev_p1_lastuse[e] = p1last
        ev_p2_lastuse[e] = p2last
        if e + 1 < NEXP:
            load_w(e + 1)

    P.emit("G", None, waits=(("dA", P.val("dA")), ("dI", P.val("dI"))))
    P.emit("G", lambda g: g.drain())

    with nc.Block() as block:
        semh = {}
        for s in SEMS:
            cm = nc.semaphore(s)
            semh[s] = cm.__enter__()
            cms.append(cm)

        def replay(eng, oplist):
            lastw = {}
            for waits, fn, inc in oplist:
                for s, v in waits:
                    if v > 0 and lastw.get(s, -1) < v:
                        eng.wait_ge(semh[s], v)
                        lastw[s] = v
                if fn is None:
                    continue
                inst = fn(eng)
                if inc is not None:
                    inst.then_inc(semh[inc[0]], inc[1])

        @block.tensor
        def _(t):
            replay(t, P.ops["T"])

        @block.vector
        def _(v):
            replay(v, P.ops["V"])

        @block.scalar
        def _(s):
            replay(s, P.ops["S"])

        @block.sync
        def _(q):
            replay(q, P.ops["Q"])

        @block.gpsimd
        def _(g):
            replay(g, P.ops["G"])

    return nc


def _host_prep(inp):
    X = np.ascontiguousarray(np.asarray(inp["L0_attn_X"], dtype=np.float32))
    Wqkv = np.asarray(inp["L0_attn_W_qkv"], dtype=np.float32)
    bqkv = np.asarray(inp["L0_attn_b_qkv"], dtype=np.float32)
    Wout = np.asarray(inp["L0_attn_W_out"], dtype=np.float32)
    bout = np.asarray(inp["L0_attn_b_out"], dtype=np.float32)
    W1 = np.asarray(inp["L0_ffn_W1"], dtype=np.float32)
    b1 = np.asarray(inp["L0_ffn_b1"], dtype=np.float32)
    W2 = np.asarray(inp["L0_ffn_W2"], dtype=np.float32)
    b2 = np.asarray(inp["L0_ffn_b2"], dtype=np.float32)
    Wr = np.asarray(inp["L0_ffn_Wr"], dtype=np.float32)
    br = np.asarray(inp["L0_ffn_br"], dtype=np.float32)

    Wsplit = Wqkv.reshape(E, H, 3, D)
    bsplit = bqkv.reshape(H, 3, D)
    shared = {
        "wq": np.ascontiguousarray(Wsplit[:, :, 0, :].reshape(E, E)),
        "wk": np.ascontiguousarray(Wsplit[:, :, 1, :].reshape(E, E)),
        "wv": np.ascontiguousarray(Wsplit[:, :, 2, :].reshape(E, E)),
        "wout": np.ascontiguousarray(Wout),
        "bqr": np.ascontiguousarray(bsplit[:, 0, :].reshape(E).reshape(8, 128).T),
        "bkr": np.ascontiguousarray(bsplit[:, 1, :].reshape(E).reshape(8, 128).T),
        "bvb": np.ascontiguousarray(
            np.broadcast_to(bsplit[:, 2, :].reshape(E), (128, E))),
        "boutb": np.ascontiguousarray(np.broadcast_to(bout, (128, E))),
        "w1": W1, "w2": W2,
        "b1r": np.ascontiguousarray(
            b1.reshape(NEXP, 32, 128).transpose(2, 0, 1).reshape(128, NEXP * 32)),
        "b2b": np.ascontiguousarray(
            np.broadcast_to(b2[:, None, :], (NEXP, 128, E))),
        "wr": np.ascontiguousarray(Wr),
        "brb": np.ascontiguousarray(np.broadcast_to(br, (128, NEXP))),
        "ident": np.eye(128, dtype=np.float32),
        "utri": np.triu(np.ones((128, 128), dtype=np.float32), 1),
        "onescol": np.ones((1, 128), dtype=np.float32),
        "onesrow": np.ones((128, 1), dtype=np.float32),
        "ecvals": np.ascontiguousarray(np.broadcast_to(
            (np.arange(NEXP, dtype=np.float32) * CAP), (128, NEXP))),
        "tokids": np.ascontiguousarray(
            (np.arange(NSLOT, dtype=np.int32)[None, :] * 128
             + np.arange(128, dtype=np.int32)[:, None])),
    }

    jj, ii = np.meshgrid(np.arange(512, dtype=np.float32),
                         np.arange(128, dtype=np.float32))
    bfull = (jj - ii).astype(np.float32)
    shared["bfull"] = bfull

    in_maps = []
    for c in range(8):
        b, p = c // 2, c % 2
        qts = 2 * np.arange(NSLOT) + p
        bdiag = np.empty((2, 128, 512), dtype=np.float32)
        for j in range(2):
            t = 256 * j + 128 * p
            bdiag[j] = np.where(bfull <= t, bfull, np.float32(MASKVAL))
        crep8 = np.zeros((H, NSLOT, 8), dtype=np.float32)
        for s in range(NSLOT):
            q0 = 128 * (2 * s + p)
            for ch in range(CNT[s]):
                crep8[:, s, ch] = 8.0 * _SLOPES * np.float32(512 * ch - q0)
        m = dict(shared)
        m["xq"] = np.ascontiguousarray(X[b].reshape(32, 128, E)[qts].reshape(NQ, E))
        m["xctx"] = np.ascontiguousarray(X[b])
        m["bdiag"] = bdiag
        m["crep8"] = np.ascontiguousarray(
            np.broadcast_to(crep8.reshape(1, -1), (128, H * NSLOT * 8)))
        in_maps.append(m)
    return in_maps


def kernel(**inp):
    in_maps = _host_prep(inp)
    nc = build_program()
    res = run_bass_kernel_spmd(nc, in_maps, core_ids=list(range(8)))
    out = np.empty((B, L, E), dtype=np.float32)
    for c in range(8):
        b, p = c // 2, c % 2
        qts = 2 * np.arange(NSLOT) + p
        y = np.asarray(res.results[c]["y"]).reshape(NSLOT, 128, E)
        out.reshape(B, 32, 128, E)[b, qts] = y
    return out
